# version 13
# baseline (speedup 1.0000x reference)
import numpy as np
from contextlib import ExitStack, contextmanager
import concourse.bass as bass
import concourse.mybir as mybir
from concourse.bass_utils import run_bass_kernel_spmd


@contextmanager
def allocs(*cms):
    with ExitStack() as st:
        yield tuple(st.enter_context(cm) for cm in cms)

F32 = mybir.dt.float32
F16 = mybir.dt.float16
AF = mybir.ActivationFunctionType
ALU = mybir.AluOpType

NCORE = 8
TC = 2048
D = 2048
FF = 4096
H = 8192
ND = 16
NTT = 16
NSUB = 4
TSUB = 512
EPS = 1e-5
RD = 1.0 / 2048.0
UW = 2051


class Plan:
    ENGINES = ("sync", "scalar", "vector", "tensor", "gpsimd")

    def __init__(self, sems):
        self.ops = {e: [] for e in self.ENGINES}
        self.cnt = {}
        self.S = sems

    def emit(self, eng, fn):
        self.ops[eng].append(fn)

    def wait(self, eng, sem, val=None):
        v = self.cnt.get(sem, 0) if val is None else val
        if v <= 0:
            return
        self.emit(eng, lambda q, s=self.S[sem], vv=v: q.wait_ge(s, vv))

    def dma(self, eng, sem, outap, inap, **kw):
        self.cnt[sem] = self.cnt.get(sem, 0) + 16
        self.emit(eng, lambda q, o=outap, i=inap, s=self.S[sem], k=kw:
                  q.dma_start(out=o, in_=i, **k).then_inc(s, 16))

    def trp(self, outap, inap, sem="tr"):
        self.cnt[sem] = self.cnt.get(sem, 0) + 16
        self.emit("sync", lambda q, o=outap, i=inap, s=self.S[sem]:
                  q.dma_start(out=o, in_=i, transpose=True).then_inc(s, 16))

    def op(self, eng, sem, fn):
        self.cnt[sem] = self.cnt.get(sem, 0) + 1
        self.emit(eng, lambda q, f=fn, s=self.S[sem]: f(q).then_inc(s, 1))

    def opn(self, eng, fn):
        self.emit(eng, fn)

    def n(self, sem):
        return self.cnt.get(sem, 0)

    def flush(self, block):
        plans = self.ops
        self.ops = {e: [] for e in self.ENGINES}

        @block.sync
        def _(q):
            for f in plans["sync"]:
                f(q)

        @block.scalar
        def _(q):
            for f in plans["scalar"]:
                f(q)

        @block.vector
        def _(q):
            for f in plans["vector"]:
                f(q)

        @block.tensor
        def _(q):
            for f in plans["tensor"]:
                f(q)

        @block.gpsimd
        def _(q):
            for f in plans["gpsimd"]:
                f(q)


def build_nc():
    nc = bass.Bass("TRN2", target_bir_lowering=False, debug=False)

    xin = nc.dram_tensor("xin", [TC, D], F32, kind="ExternalInput").ap()
    xhalo = nc.dram_tensor("xhalo", [16, D], F32, kind="ExternalInput").ap()
    w_inT = nc.dram_tensor("w_inT", [128, 32 * 2048], F32, kind="ExternalInput").ap()
    w_xpT = nc.dram_tensor("w_xpT", [128, 16 * 160], F32, kind="ExternalInput").ap()
    w_dtT = nc.dram_tensor("w_dtT", [128, D], F32, kind="ExternalInput").ap()
    w_outT = nc.dram_tensor("w_outT", [128, 32 * 1024], F32, kind="ExternalInput").ap()
    w0T = nc.dram_tensor("w0T", [128, 64 * 2048], F32, kind="ExternalInput").ap()
    w1T = nc.dram_tensor("w1T", [128, 16 * 8192], F32, kind="ExternalInput").ap()
    c_alog = nc.dram_tensor("c_alog", [128, 256], F32, kind="ExternalInput").ap()
    c_cw = nc.dram_tensor("c_cw", [128, 64], F32, kind="ExternalInput").ap()
    c_cb = nc.dram_tensor("c_cb", [128, 16], F32, kind="ExternalInput").ap()
    c_bdt = nc.dram_tensor("c_bdt", [128, 16], F32, kind="ExternalInput").ap()
    c_ds = nc.dram_tensor("c_ds", [128, 16], F32, kind="ExternalInput").ap()
    c_b0 = nc.dram_tensor("c_b0", [128, 64], F32, kind="ExternalInput").ap()
    c_b1 = nc.dram_tensor("c_b1", [128, 16], F32, kind="ExternalInput").ap()
    c_csel = nc.dram_tensor("c_csel", [128, 8], F32, kind="ExternalInput").ap()
    out = nc.dram_tensor("out", [TC, D], F32, kind="ExternalOutput").ap()

    sp_u = nc.dram_tensor("sp_u", [ND, 128, TC], F16).ap()
    sp_z = nc.dram_tensor("sp_z", [ND, 128, TC], F16).ap()
    sp_dt = nc.dram_tensor("sp_dt", [ND, 128, TC], F16).ap()
    sp_yh = nc.dram_tensor("sp_yh", [ND, 128, TC], F16).ap()
    sp_D = nc.dram_tensor("sp_D", [ND, 128, TC], F32).ap()
    sp_C = nc.dram_tensor("sp_C", [16, TC], F16).ap()
    sp_yp = nc.dram_tensor("sp_yp", [ND, 128, TC], F16).ap()
    sp_x1 = nc.dram_tensor("sp_x1", [NTT, 128, D], F16).ap()
    sp_E = nc.dram_tensor("sp_E", [16, 128, TC], F16).ap()
    cc_in = nc.dram_tensor("cc_in", [D, 32], F32).ap()
    cc_out = nc.dram_tensor("cc_out", [NCORE * D, 32], F32).ap()

    with allocs(
        nc.semaphore("s_lc"),
        nc.semaphore("s_lx"),
        nc.semaphore("s_tr"),
        nc.semaphore("s_wt"),
        nc.semaphore("s_sp"),
        nc.semaphore("s_rd"),
        nc.semaphore("s_ea"),
        nc.semaphore("s_ou"),
        nc.semaphore("s_cc"),
        nc.semaphore("s_pe"),
        nc.semaphore("s_act"),
        nc.semaphore("s_dve"),
        nc.semaphore("s_gpc"),
        nc.semaphore("s_lx1"),
        nc.semaphore("s_wt1"),
        nc.semaphore("s_sp1"),
        nc.semaphore("s_rd1"),
        nc.semaphore("s_ea1"),
        nc.semaphore("s_tr1"),
        nc.semaphore("s_xr"),
        nc.semaphore("s_xr1"),
    ) as (s_lc, s_lx, s_tr, s_wt, s_sp, s_rd, s_ea, s_ou, s_cc, s_pe, s_act, s_dve, s_gpc, s_lx1, s_wt1, s_sp1, s_rd1, s_ea1, s_tr1, s_xr, s_xr1,):
        S = dict(lc=s_lc, lx=s_lx, tr=s_tr, wt=s_wt, sp=s_sp, rd=s_rd,
                 ea=s_ea, ou=s_ou, cc=s_cc, pe=s_pe, act=s_act, dve=s_dve,
                 gpc=s_gpc, lx1=s_lx1, wt1=s_wt1, sp1=s_sp1, rd1=s_rd1,
                 ea1=s_ea1, tr1=s_tr1, xr=s_xr, xr1=s_xr1)

        def semp(base, par):
            return base if par % 2 == 0 else base + "1"
        p = Plan(S)

        with allocs(
            nc.sbuf_tensor("cA", [128, 256], F32),
            nc.sbuf_tensor("cCW", [128, 64], F32),
            nc.sbuf_tensor("cCB", [128, 16], F32),
            nc.sbuf_tensor("cBD", [128, 16], F32),
            nc.sbuf_tensor("cDS", [128, 16], F32),
            nc.sbuf_tensor("cB0", [128, 64], F32),
            nc.sbuf_tensor("cB1", [128, 16], F32),
            nc.sbuf_tensor("cSELQ", [128, 8], F32),
            nc.sbuf_tensor("tSEL", [16, 2048], F16),
            nc.sbuf_tensor("tJA", [128, 2048], F16),
            nc.sbuf_tensor("tX", [128, 2 * 2048], F16),
            nc.sbuf_tensor("tLNTM", [128, 2048], F16),
            nc.sbuf_tensor("tLNHD", [128, 2048], F16),
            nc.sbuf_tensor("tW", [128, 2 * 4096], F16),
            nc.sbuf_tensor("tOUT", [128, 2 * 2048], F32),
            nc.sbuf_tensor("tSTAT", [128, 8], F32),
            nc.sbuf_tensor("cKON", [128, 6], F32),
            nc.sbuf_tensor("carry", [128, 256], F32),
            nc.sbuf_tensor("dcarry", [128, 16], F32),
            nc.sbuf_tensor("hinA", [128, 256], F32),
            nc.sbuf_tensor("hinB", [128, 256], F32),
            nc.sbuf_tensor("tBB", [16, 2048], F16),
            nc.sbuf_tensor("tCCt", [16, 2048], F16),
            nc.sbuf_tensor("BIGA", [128, 16 * 2048], F16),
            nc.Block(),
        ) as (cA, cCW, cCB, cBD, cDS, cB0, cB1, cSELQ, tSEL, tJA, tX, tLNTM, tLNHD, tW, tOUT, tSTAT, cKON, carry, dcarry, hinA, hinB, tBB, tCCt, BIGA, block,):
            _pro = ExitStack()
            tXH = _pro.enter_context(nc.sbuf_tensor("tXH", [16, 2048], F32))
            tLNH = _pro.enter_context(nc.sbuf_tensor("tLNH", [128, 2048], F16))
            tIOT = _pro.enter_context(nc.sbuf_tensor("tIOT", [16, 2048], F32))
            cAe = _pro.enter_context(nc.sbuf_tensor("cAe", [128, 256], F32))
            if True:
                for dst, src in [(cAe, c_alog), (cCW, c_cw), (cCB, c_cb),
                                 (cBD, c_bdt), (cDS, c_ds), (cB0, c_b0),
                                 (cB1, c_b1), (cSELQ, c_csel)]:
                    p.dma("sync", "lc", dst[:], src[:, :])
                p.dma("sync", "lc", tXH[:], xhalo[:, :])
                p.op("gpsimd", "gpc", lambda q: q.iota(
                    tIOT[:], pattern=[[1, 16], [0, 128]], base=0,
                    channel_multiplier=-1,
                    allow_small_or_imprecise_dtypes=True))
                p.wait("vector", "gpc")
                p.op("vector", "dve", lambda q: q.tensor_scalar(
                    tSEL[:], tIOT[:], 0.0, None, ALU.is_equal))
                p.op("vector", "dve", lambda q: q.memset(cKON[:, 0:1], float(EPS)))
                p.op("vector", "dve", lambda q: q.memset(cKON[:, 1:2], 1.0))
                p.op("vector", "dve", lambda q: q.memset(cKON[:, 2:3], -0.5))
                p.op("vector", "dve", lambda q: q.memset(cKON[:, 3:4], -1.0))
                p.op("vector", "dve", lambda q: q.memset(cKON[:, 4:5], float(2048.0 ** -0.5)))
                for t in (carry, dcarry, hinA, hinB):
                    p.op("vector", "dve", lambda q, t=t: q.memset(t[:], 0.0))
                p.wait("scalar", "lc")
                p.op("scalar", "act", lambda q: q.activation(cA[:], cAe[:], AF.Exp))
                p.wait("vector", "act")
                p.op("vector", "dve", lambda q: q.tensor_scalar(
                    cA[:], cA[:], -1.0, None, ALU.mult))
                p.wait("vector", "lc")
                p.wait("scalar", "dve")
                p.op("vector", "dve", lambda q: q.reduce_sum(
                    tSTAT[0:16, 0:1], tXH[:], axis=mybir.AxisListType.X))
                p.op("scalar", "act", lambda q: q.activation(
                    tJA[0:16, :], tXH[:], AF.Square, scale=cKON[0:16, 4:5]))
                p.wait("vector", "act")
                p.op("vector", "dve", lambda q: q.reduce_sum(
                    tSTAT[0:16, 1:2], tJA[0:16, :], axis=mybir.AxisListType.X))
                p.op("vector", "dve", lambda q: q.tensor_scalar(
                    tSTAT[0:16, 2:3], tSTAT[0:16, 0:1], RD, None, ALU.mult))
                p.wait("vector", "dve")
                p.op("vector", "dve", lambda q: q.scalar_tensor_tensor(
                    tSTAT[0:16, 4:5], tSTAT[0:16, 2:3], tSTAT[0:16, 2:3],
                    tSTAT[0:16, 1:2], ALU.mult, ALU.subtract))
                p.wait("scalar", "dve")
                p.op("scalar", "act", lambda q: q.activation(
                    tSTAT[0:16, 5:6], tSTAT[0:16, 4:5], AF.Ln,
                    scale=cKON[0:16, 3:4], bias=cKON[0:16, 0:1]))
                p.wait("scalar", "act")
                p.op("scalar", "act", lambda q: q.activation(
                    tSTAT[0:16, 6:7], tSTAT[0:16, 5:6], AF.Exp, scale=cKON[0:16, 2:3]))
                p.wait("vector", "act")
                p.opn("vector", lambda q: q.memset(tLNH[:], 0.0))
                p.op("vector", "dve", lambda q: q.tensor_scalar(
                    tLNH[0:16, :], tXH[:], tSTAT[0:16, 2:3], tSTAT[0:16, 6:7],
                    ALU.subtract, ALU.mult))
                p.wait("sync", "dve")
                for j in range(ND):
                    p.trp(tLNHD[:, j * 128:(j + 1) * 128],
                          tLNH[:, j * 128:(j + 1) * 128])

                mk_tr_tile = {}
                for i in range(NTT):
                    par = i % 2
                    xs = tX[:, par * 2048:(par + 1) * 2048]
                    if i >= 2:
                        p.wait("gpsimd", "dve")
                    p.dma("gpsimd", semp("lx", i), xs,
                          xin[i * 128:(i + 1) * 128, :])
                    p.wait("vector", semp("lx", i))
                    p.wait("scalar", semp("lx", i))
                    p.op("vector", "dve", lambda q, xs=xs: q.reduce_sum(
                        tSTAT[:, 0:1], xs, axis=mybir.AxisListType.X))
                    p.op("scalar", "act", lambda q, xs=xs: q.activation(
                        tJA[:], xs, AF.Square, scale=cKON[:, 4:5]))
                    p.wait("vector", "act")
                    p.op("vector", "dve", lambda q: q.reduce_sum(
                        tSTAT[:, 1:2], tJA[:], axis=mybir.AxisListType.X))
                    p.op("vector", "dve", lambda q: q.tensor_scalar(
                        tSTAT[:, 2:3], tSTAT[:, 0:1], RD, None, ALU.mult))
                    p.wait("vector", "dve")
                    p.op("vector", "dve", lambda q: q.scalar_tensor_tensor(
                        tSTAT[:, 4:5], tSTAT[:, 2:3], tSTAT[:, 2:3],
                        tSTAT[:, 1:2], ALU.mult, ALU.subtract))
                    p.wait("scalar", "dve")
                    p.op("scalar", "act", lambda q: q.activation(
                        tSTAT[:, 5:6], tSTAT[:, 4:5], AF.Ln,
                    scale=cKON[:, 3:4], bias=cKON[:, 0:1]))
                    p.wait("scalar", "act")
                    p.op("scalar", "act", lambda q: q.activation(
                        tSTAT[:, 6:7], tSTAT[:, 5:6], AF.Exp, scale=cKON[:, 2:3]))
                    p.wait("vector", "act")
                    if i >= 1:
                        p.wait("vector", "tr")
                    p.op("vector", "dve", lambda q, xs=xs:
                         q.tensor_scalar(
                             tLNTM[:, 0:2048], xs,
                             tSTAT[:, 2:3], tSTAT[:, 6:7],
                             ALU.subtract, ALU.mult))
                    p.wait("sync", "dve")
                    for j in range(ND):
                        p.trp(BIGA[:, j * 2048 + i * 128:
                                   j * 2048 + (i + 1) * 128],
                              tLNTM[:, j * 128:(j + 1) * 128])
                    mk_tr_tile[i] = p.n("tr")
                p.flush(block)
                _pro.close()

                with allocs(
                    nc.sbuf_tensor("tDTR", [128, 2048], F16),
                    nc.psum_tensor("psW", [128, 2048], F32),
                    nc.psum_tensor("psA", [128, 512], F32),
                    nc.psum_tensor("psB", [128, 512], F32),
                    nc.psum_tensor("psH", [128, 16], F32),
                ) as (tDTR, psW, psA, psB, psH,):
                    _bd = ExitStack()
                    tU = _bd.enter_context(
                        nc.sbuf_tensor("tU", [128, 16 * UW], F16))
                    tZS = _bd.enter_context(
                        nc.sbuf_tensor("tZS", [128, 2048], F16))
                    evmk = {}
                    pemk = {}
                    qev = {}
                    qpe = {}
                    for f in range(32):
                        par = f % 2
                        ws = tW[:, par * 4096: par * 4096 + 2048]
                        if f >= 2:
                            p.wait("gpsimd", "pe", pemk[f - 2])
                        p.dma("gpsimd", semp("wt", f), ws,
                              w_inT[:, f * 2048:(f + 1) * 2048])
                        p.wait("tensor", semp("wt", f))
                        for quad in range(4):
                            if f == 0:
                                p.wait("tensor", "tr",
                                       mk_tr_tile[quad * 4 + 3])
                            if f >= 1:
                                p.wait("tensor", "act", qev[(f - 1, quad)])
                            for k in range(ND):
                                mmfn = (lambda q, k=k, quad=quad, ws=ws:
                                        q.matmul(
                                            psW[:, quad * 512:(quad + 1) * 512],
                                            ws[:, k * 128:(k + 1) * 128],
                                            BIGA[:, k * 2048 + quad * 512:
                                                 k * 2048 + (quad + 1) * 512],
                                            start=(k == 0), stop=(k == ND - 1)))
                                if k == ND - 1:
                                    p.op("tensor", "pe", mmfn)
                                else:
                                    p.opn("tensor", mmfn)
                            qpe[(f, quad)] = p.n("pe")
                        if f < ND:
                            if f >= 1:
                                p.wait("tensor", "act", qev[(f - 1, 4)])
                            for k in range(ND):
                                mmfn = (lambda q, k=k, ws=ws: q.matmul(
                                    psH[:], ws[:, k * 128:(k + 1) * 128],
                                    tLNHD[:, k * 128: k * 128 + 16],
                                    start=(k == 0), stop=(k == ND - 1)))
                                if k == ND - 1:
                                    p.op("tensor", "pe", mmfn)
                                else:
                                    p.opn("tensor", mmfn)
                        pemk[f] = p.n("pe")
                        if f < ND:
                            for quad in range(4):
                                p.wait("scalar", "pe", qpe[(f, quad)])
                                p.op("scalar", "act",
                                     lambda q, f=f, quad=quad: q.copy(
                                         tU[:, f * UW + 3 + quad * 512:
                                            f * UW + 3 + (quad + 1) * 512],
                                         psW[:, quad * 512:(quad + 1) * 512]))
                                qev[(f, quad)] = p.n("act")
                            p.wait("scalar", "pe", pemk[f])
                            p.op("scalar", "act", lambda q, f=f: q.copy(
                                tU[:, f * UW: f * UW + 3], psH[:, 13:16]))
                            qev[(f, 4)] = p.n("act")
                        else:
                            p.wait("scalar", "pe", pemk[f])
                            if f >= 18:
                                p.wait("scalar", semp("sp", f))
                            p.op("scalar", "act", lambda q: q.copy(
                                tZS[:], psW[:]))
                            for quad in range(4):
                                qev[(f, quad)] = p.n("act")
                            p.wait("scalar", "act")
                            p.dma("scalar", semp("sp", f), sp_z[f - 16, :, :],
                                  tZS[:])
                        evmk[f] = p.n("act")

                    for j in range(ND):
                        u0 = tU[:, j * UW + 0: j * UW + 2048]
                        u1 = tU[:, j * UW + 1: j * UW + 2049]
                        u2 = tU[:, j * UW + 2: j * UW + 2050]
                        u3 = tU[:, j * UW + 3: j * UW + 2051]
                        ca = tX[:, 0:2048]
                        cb = tX[:, 2048:4096]
                        p.wait("vector", "act", evmk[j])
                        if j >= 1:
                            p.wait("vector", "act")
                        p.op("vector", "dve", lambda q, u0=u0, ca=ca, j=j:
                             q.tensor_scalar(ca, u0,
                                             cCW[:, j * 4: j * 4 + 1],
                                             None, ALU.mult))
                        p.op("vector", "dve", lambda q, u1=u1, ca=ca, cb=cb, j=j:
                             q.scalar_tensor_tensor(
                                 cb, u1, cCW[:, j * 4 + 1: j * 4 + 2], ca,
                                 ALU.mult, ALU.add))
                        p.op("vector", "dve", lambda q, u2=u2, ca=ca, cb=cb, j=j:
                             q.scalar_tensor_tensor(
                                 ca, u2, cCW[:, j * 4 + 2: j * 4 + 3], cb,
                                 ALU.mult, ALU.add))
                        p.op("vector", "dve", lambda q, u3=u3, ca=ca, cb=cb, j=j:
                             q.scalar_tensor_tensor(
                                 cb, u3, cCW[:, j * 4 + 3: j * 4 + 4], ca,
                                 ALU.mult, ALU.add))
                        p.wait("scalar", "dve")
                        p.op("scalar", "act", lambda q, u3=u3, cb=cb, j=j:
                             q.activation(u3, cb, AF.Silu,
                                          bias=cCB[:, j: j + 1]))
                        p.wait("scalar", "act")
                        p.dma("scalar", semp("sp", j), sp_u[j, :, :], u3)
                    uconv_mk = p.n("act")

                    p.wait("gpsimd", "pe", pemk[31])
                    p.dma("gpsimd", "wt", tW[:, 0:2560], w_xpT[:, :])
                    wxp = tW[:, 0:2560]
                    p.wait("tensor", "wt")
                    p.wait("tensor", "wt1")
                    p.wait("tensor", "act", uconv_mk)
                    for quad in range(4):
                        for k in range(ND):
                            mmfn = (lambda q, k=k, quad=quad: q.matmul(
                                psW[:, quad * 512:(quad + 1) * 512],
                                wxp[:, k * 160: k * 160 + 128],
                                tU[:, k * UW + 3 + quad * 512:
                                   k * UW + 3 + (quad + 1) * 512],
                                start=(k == 0), stop=(k == ND - 1)))
                            if quad == 3 and k == ND - 1:
                                p.op("tensor", "pe", mmfn)
                            else:
                                p.opn("tensor", mmfn)
                    p.wait("scalar", "pe")
                    p.op("scalar", "act", lambda q: q.copy(tDTR[:], psW[:]))
                    for tgt, coff in ((tBB, 128), (tCCt, 144)):
                        for quad in range(4):
                            ps = psA if quad % 2 == 0 else psB
                            if not (tgt is tBB and quad == 0):
                                p.wait("tensor", "act")
                            for k in range(ND):
                                mmfn = (lambda q, k=k, quad=quad, ps=ps,
                                        coff=coff:
                                        q.matmul(
                                            ps[0:16, :],
                                            wxp[:, k * 160 + coff:
                                                k * 160 + coff + 16],
                                            tU[:, k * UW + 3 + quad * 512:
                                               k * UW + 3 + (quad + 1) * 512],
                                            start=(k == 0), stop=(k == ND - 1)))
                                if k == ND - 1:
                                    p.op("tensor", "pe", mmfn)
                                else:
                                    p.opn("tensor", mmfn)
                            p.wait("scalar", "pe")
                            p.op("scalar", "act",
                                 lambda q, quad=quad, ps=ps, tgt=tgt: q.copy(
                                     tgt[:, quad * 512:(quad + 1) * 512],
                                     ps[0:16, :]))
                    p.wait("scalar", "act")
                    p.dma("scalar", "sp", sp_C[:, :], tCCt[:])
                    p.wait("vector", "pe")
                    p.flush(block)
                    _bd.close()

                    with allocs(nc.sbuf_tensor("tDTw", [128, 2 * 2048], F16), nc.sbuf_tensor("tSP1", [128, 2048], F16), nc.sbuf_tensor("tDF", [128, 2048], F32), nc.sbuf_tensor("tONED2", [128, 2048], F16)) as (tDTw, tSP1, tDF, tONED2,):
                        p.opn("vector", lambda q: q.memset(tONED2[:], 1.0))
                        p.wait("scalar", "sp")
                        p.wait("vector", "sp")
                        p.dma("gpsimd", "wt1", tW[:, 4096:4096 + 2048],
                              w_dtT[:, :])
                        wdt = tW[:, 4096:4096 + 2048]
                        p.wait("tensor", "wt1")
                        for j in range(ND):
                            par = j % 2
                            dts = tDTw[:, par * 2048:(par + 1) * 2048]
                            p.wait("tensor", "act")
                            for quad in range(4):
                                mmfn = (lambda q, j=j, quad=quad:
                                        q.matmul(
                                            psW[:, quad * 512:(quad + 1) * 512],
                                            wdt[:, j * 128:(j + 1) * 128],
                                            tDTR[:, quad * 512:(quad + 1) * 512],
                                            start=True, stop=True))
                                if quad == 3:
                                    p.op("tensor", "pe", mmfn)
                                else:
                                    p.opn("tensor", mmfn)
                            p.wait("scalar", "pe")
                            if j >= 2:
                                p.wait("scalar", semp("sp", j))
                            p.op("scalar", "act", lambda q, j=j: q.activation(
                                tSP1[:], psW[:], AF.Exp,
                                bias=cBD[:, j: j + 1]))
                            p.op("scalar", "act", lambda q, dts=dts:
                                 q.activation(dts, tSP1[:], AF.Ln, bias=cKON[:, 1:2]))
                            mk_dt = p.n("act")
                            p.wait("scalar", "act")
                            p.dma("scalar", semp("sp", j), sp_dt[j, :, :], dts)
                            p.wait("vector", "act", mk_dt)
                            if j >= 1:
                                p.wait("vector", semp("sp", j - 1))
                            p.op("vector", "dve", lambda q, dts=dts:
                                 q.tensor_tensor_scan(
                                     tDF[:], tONED2[:], dts, 0.0,
                                     ALU.mult, ALU.add))
                            p.op("vector", "dve", lambda q, j=j: q.tensor_copy(
                                dcarry[:, j: j + 1], tDF[:, 2047:2048]))
                            p.wait("scalar", "dve")
                            p.dma("scalar", semp("sp", j), sp_D[j, :, :], tDF[:])
                        p.flush(block)

                with allocs(
                    nc.sbuf_tensor("tBC", [128, 32 * 512], F16),
                    nc.sbuf_tensor("tUS", [128, 2 * 512], F16),
                    nc.sbuf_tensor("tDTs", [128, 2 * 512], F16),
                    nc.sbuf_tensor("tDTU", [128, 512], F16),
                    nc.sbuf_tensor("tDA", [128, 2 * 512], F16),
                    nc.sbuf_tensor("tDBU", [128, 2 * 512], F16),
                    nc.sbuf_tensor("tHH", [128, 2 * 512], F16),
                    nc.sbuf_tensor("tHC", [128, 2 * 512], F16),
                    nc.sbuf_tensor("tYH", [128, 2 * 512], F16),
                ) as (tBC, tUS, tDTs, tDTU, tDA, tDBU, tHH, tHC, tYH,):
                    p.wait("scalar", "dve")
                    p.wait("scalar", "sp")
                    p.wait("scalar", "sp1")
                    mk_dA = {}
                    mk_dtu = {}
                    mk_scan = {}
                    mk_spyh = {}
                    mk_hc = {}
                    mk_gpa = {}
                    mk_gpy = {}
                    mk_y0 = {}
                    for sub in range(NSUB):
                        t0 = sub * TSUB
                        if sub >= 1:
                            p.wait("scalar", "dve")
                        for half, src in ((0, tBB), (1, tCCt)):
                            for n in range(16):
                                ps = psA if n % 2 == 0 else psB
                                p.wait("tensor", "act")
                                p.op("tensor", "pe",
                                     lambda q, n=n, t0=t0, ps=ps, src=src:
                                     q.matmul(
                                         ps[:],
                                         tSEL[:, n * 128:(n + 1) * 128],
                                         src[:, t0:t0 + 512],
                                         start=True, stop=True))
                                p.wait("scalar", "pe")
                                p.op("scalar", "act",
                                     lambda q, n=n, half=half, ps=ps: q.copy(
                                         tBC[:, (half * 16 + n) * 512:
                                             (half * 16 + n + 1) * 512],
                                         ps[:]))
                        mk_bc = p.n("act")

                        for j in range(ND):
                            upar = j % 2
                            us = tUS[:, upar * 512:(upar + 1) * 512]
                            dts = tDTs[:, upar * 512:(upar + 1) * 512]
                            if j >= 2:
                                p.wait("scalar", "act", mk_dA[(sub, j - 2, 15)])
                                p.wait("scalar", "dve", mk_dtu[(sub, j - 2)])
                            p.dma("scalar", semp("rd", j), us,
                                  sp_u[j, :, t0:t0 + 512])
                            p.dma("scalar", semp("rd", j), dts,
                                  sp_dt[j, :, t0:t0 + 512])
                            p.wait("vector", semp("rd", j))
                            p.wait("vector", "act", mk_bc)
                            if (sub, j - 2) in mk_spyh:
                                p.wait("vector", semp("sp", j),
                                       mk_spyh[(sub, j - 2)])
                            elif sub >= 1 and j < 2:
                                p.wait("vector", semp("sp", j),
                                       mk_spyh[(sub - 1, 14 + j)])
                            p.op("vector", "dve", lambda q, us=us, dts=dts:
                                 q.tensor_mul(tDTU[:], dts, us))
                            mk_dtu[(sub, j)] = p.n("dve")
                            ypar = j % 2
                            for n in range(16):
                                hpar = n % 2
                                if n == 0:
                                    p.wait("scalar", semp("rd", j))
                                if n >= 2:
                                    p.wait("scalar", "dve",
                                           mk_scan[(sub, j, n - 2)])
                                elif j >= 1 and n < 2:
                                    p.wait("scalar", "dve",
                                           mk_scan[(sub, j - 1, 14 + n)])
                                elif sub >= 1 and j == 0 and n < 2:
                                    p.wait("scalar", "dve",
                                           mk_scan[(sub - 1, 15, 14 + n)])
                                p.op("scalar", "act",
                                     lambda q, j=j, n=n, hpar=hpar, dts=dts:
                                     q.activation(
                                         tDA[:, hpar * 512:(hpar + 1) * 512],
                                         dts, AF.Exp,
                                         scale=cA[:, j * 16 + n:
                                                  j * 16 + n + 1]))
                                mk_dA[(sub, j, n)] = p.n("act")
                                p.op("vector", "dve", lambda q, n=n, hpar=hpar:
                                     q.tensor_mul(
                                         tDBU[:, hpar * 512:(hpar + 1) * 512],
                                         tDTU[:],
                                         tBC[:, n * 512:(n + 1) * 512]))
                                p.wait("vector", "act", mk_dA[(sub, j, n)])
                                p.op("vector", "dve",
                                     lambda q, j=j, n=n, hpar=hpar:
                                     q.tensor_tensor_scan(
                                         tHH[:, hpar * 512:(hpar + 1) * 512],
                                         tDA[:, hpar * 512:(hpar + 1) * 512],
                                         tDBU[:, hpar * 512:(hpar + 1) * 512],
                                         carry[:, j * 16 + n: j * 16 + n + 1],
                                         ALU.mult, ALU.add))
                                mk_scan[(sub, j, n)] = p.n("dve")
                                p.op("vector", "dve",
                                     lambda q, j=j, n=n, hpar=hpar:
                                     q.tensor_copy(
                                         carry[:, j * 16 + n: j * 16 + n + 1],
                                         tHH[:, hpar * 512 + 511:
                                             hpar * 512 + 512]))
                                if n == 0:
                                    if (sub, j - 2) in mk_gpy:
                                        p.wait("vector", "gpc",
                                               mk_gpy[(sub, j - 2)])
                                    elif sub >= 1 and j < 2:
                                        p.wait("vector", "gpc",
                                               mk_gpy[(sub - 1, 14 + j)])
                                    p.op("vector", "dve",
                                         lambda q, hpar=hpar, ypar=ypar:
                                         q.tensor_mul(
                                             tYH[:, ypar * 512:
                                                 (ypar + 1) * 512],
                                             tHH[:, hpar * 512:
                                                 (hpar + 1) * 512],
                                             tBC[:, 16 * 512: 17 * 512]))
                                    mk_y0[(sub, j)] = p.n("dve")
                                else:
                                    if n >= 3:
                                        p.wait("vector", "gpc",
                                               mk_gpa[(sub, j, n - 2)])
                                    p.op("vector", "dve",
                                         lambda q, n=n, hpar=hpar:
                                         q.tensor_mul(
                                             tHC[:, (n % 2) * 512:
                                                 (n % 2 + 1) * 512],
                                             tHH[:, hpar * 512:
                                                 (hpar + 1) * 512],
                                             tBC[:, (16 + n) * 512:
                                                 (17 + n) * 512]))
                                    mk_hc[(sub, j, n)] = p.n("dve")
                                    if n == 1:
                                        p.wait("gpsimd", "dve",
                                               mk_y0[(sub, j)])
                                    p.wait("gpsimd", "dve", mk_hc[(sub, j, n)])
                                    p.op("gpsimd", "gpc",
                                         lambda q, n=n, ypar=ypar:
                                         q.tensor_add(
                                             tYH[:, ypar * 512:
                                                 (ypar + 1) * 512],
                                             tYH[:, ypar * 512:
                                                 (ypar + 1) * 512],
                                             tHC[:, (n % 2) * 512:
                                                 (n % 2 + 1) * 512]))
                                    mk_gpa[(sub, j, n)] = p.n("gpc")
                            mk_gpy[(sub, j)] = p.n("gpc")
                            p.wait("scalar", "gpc", mk_gpy[(sub, j)])
                            p.dma("scalar", semp("sp", j),
                                  sp_yh[j, :, t0:t0 + 512],
                                  tYH[:, ypar * 512:(ypar + 1) * 512])
                            mk_spyh[(sub, j)] = p.n(semp("sp", j))

                    with allocs(nc.sbuf_tensor("tP16", [128, 16], F32), nc.sbuf_tensor("tPP", [128, 256], F32)) as (tP16, tPP,):
                        for j in range(ND):
                            p.wait("vector", "act")
                            p.op("vector", "dve", lambda q, j=j: q.tensor_scalar(
                                tP16[:], cA[:, j * 16:(j + 1) * 16],
                                dcarry[:, j: j + 1], None, ALU.mult))
                            p.wait("scalar", "dve")
                            p.op("scalar", "act", lambda q, j=j: q.activation(
                                tPP[:, j * 16:(j + 1) * 16], tP16[:], AF.Exp))
                        p.wait("scalar", "act")
                        p.wait("scalar", "dve")
                        for j in range(ND):
                            p.dma("scalar", "sp",
                                  cc_in[j * 128:(j + 1) * 128, 0:16],
                                  tPP[:, j * 16:(j + 1) * 16])
                            p.dma("scalar", "sp",
                                  cc_in[j * 128:(j + 1) * 128, 16:32],
                                  carry[:, j * 16:(j + 1) * 16])
                        p.wait("gpsimd", "sp")
                        p.wait("gpsimd", "sp1")
                        p.cnt["cc"] = 1
                        p.emit("gpsimd", lambda q: q.collective_compute(
                            "AllGather", ALU.bypass,
                            replica_groups=[list(range(NCORE))],
                            ins=[cc_in[:, :].opt()],
                            outs=[cc_out[:, :].opt()]).then_inc(S["cc"], 1))
                        p.flush(block)

            with allocs(nc.sbuf_tensor("tCCR", [128, 8 * 512], F32), nc.sbuf_tensor("tGG", [128, 256], F32), nc.sbuf_tensor("tGT", [128, 256], F32)) as (tCCR, tGG, tGT,):
                p.wait("sync", "cc", 1)
                for r in range(NCORE):
                    p.dma("sync", "lc",
                          tCCR[:, r * 512:(r + 1) * 512].rearrange(
                              "p (j c) -> p j c", c=32),
                          cc_out[r * D:(r + 1) * D, :].rearrange(
                              "(j p) c -> p j c", p=128))
                p.wait("vector", "lc")

                def pview(r):
                    return tCCR[:, r * 512:(r + 1) * 512].rearrange(
                        "p (j c) -> p j c", c=32)[:, :, 0:16]

                def sview(r):
                    return tCCR[:, r * 512:(r + 1) * 512].rearrange(
                        "p (j c) -> p j c", c=32)[:, :, 16:32]

                g3 = tGG[:].rearrange("p (j c) -> p j c", c=16)
                t3 = tGT[:].rearrange("p (j c) -> p j c", c=16)
                p.op("vector", "dve", lambda q: q.tensor_copy(g3, sview(0)))
                p.op("vector", "dve", lambda q: q.tensor_scalar(
                    hinA[:], tGG[:], cSELQ[:, 0:1], None, ALU.mult))
                hin, other = hinA, hinB
                for r in range(1, 7):
                    p.op("vector", "dve", lambda q, r=r: q.tensor_tensor(
                        t3, pview(r), g3, ALU.mult))
                    p.op("vector", "dve", lambda q, r=r: q.tensor_tensor(
                        g3, sview(r), t3, ALU.add))
                    p.op("vector", "dve", lambda q, r=r, src=hin, dst=other:
                         q.scalar_tensor_tensor(
                             dst[:], tGG[:], cSELQ[:, r: r + 1], src[:],
                             ALU.mult, ALU.add))
                    hin, other = other, hin
                HIN = hin
                p.flush(block)

            p.wait("scalar", "sp")
            p.wait("scalar", "sp1")
            with allocs(
                nc.sbuf_tensor("tC2", [128, 16 * 512], F16),
                nc.sbuf_tensor("tCC16", [16, 512], F16),
                nc.sbuf_tensor("tYH2", [128, 2 * 512], F16),
                nc.sbuf_tensor("tD2", [128, 2 * 512], F32),
                nc.sbuf_tensor("tU2", [128, 2 * 512], F16),
                nc.sbuf_tensor("tZ2", [128, 2 * 512], F16),
                nc.sbuf_tensor("tCD", [128, 2 * 512], F16),
                nc.sbuf_tensor("tT1", [128, 512], F16),
                nc.sbuf_tensor("tT2", [128, 512], F16),
                nc.sbuf_tensor("tSZ", [128, 2 * 512], F16),
                nc.sbuf_tensor("tYP", [128, 2 * 512], F16),
                nc.psum_tensor("psA2", [128, 512], F32),
                nc.psum_tensor("psB2", [128, 512], F32),
            ) as (tC2, tCC16, tYH2, tD2, tU2, tZ2, tCD, tT1, tT2, tSZ, tYP, psA2, psB2,):
                mk_t1 = {}
                mk_yp2 = {}
                mk_spyp = {}
                for sub in range(NSUB):
                    t0 = sub * TSUB
                    if sub == 0:
                        p.wait("scalar", "dve")
                    p.dma("scalar", "lc", tCC16[:], sp_C[:, t0:t0 + 512])
                    p.wait("tensor", "lc")
                    for n in range(16):
                        ps = psA2 if n % 2 == 0 else psB2
                        if n >= 2:
                            p.wait("tensor", "act")
                        p.op("tensor", "pe", lambda q, n=n, ps=ps:
                             q.matmul(
                                 ps[:], tSEL[:, n * 128:(n + 1) * 128],
                                 tCC16[:], start=True, stop=True))
                        p.wait("scalar", "pe")
                        p.op("scalar", "act", lambda q, n=n, ps=ps: q.copy(
                            tC2[:, n * 512:(n + 1) * 512], ps[:]))
                    mk_c2 = p.n("act")

                    for j in range(ND):
                        par = j % 2
                        yh = tYH2[:, par * 512:(par + 1) * 512]
                        dd = tD2[:, par * 512:(par + 1) * 512]
                        uu = tU2[:, par * 512:(par + 1) * 512]
                        zz = tZ2[:, par * 512:(par + 1) * 512]
                        sz = tSZ[:, par * 512:(par + 1) * 512]
                        if (sub, j - 2) in mk_yp2:
                            p.wait("scalar", "dve", mk_yp2[(sub, j - 2)])
                            p.wait("scalar", semp("sp", j),
                                   mk_spyp[(sub, j - 2)])
                        elif sub >= 1 and j < 2:
                            p.wait("scalar", "dve",
                                   mk_yp2[(sub - 1, 14 + j)])
                            p.wait("scalar", semp("sp", j),
                                   mk_spyp[(sub - 1, 14 + j)])
                        p.dma("scalar", semp("rd", j), yh,
                              sp_yh[j, :, t0:t0 + 512])
                        p.dma("scalar", semp("rd", j), dd,
                              sp_D[j, :, t0:t0 + 512])
                        p.dma("scalar", semp("rd", j), uu,
                              sp_u[j, :, t0:t0 + 512])
                        p.dma("scalar", semp("rd", j), zz,
                              sp_z[j, :, t0:t0 + 512])
                        p.wait("scalar", semp("rd", j))
                        p.wait("vector", semp("rd", j))
                        p.wait("vector", "act", mk_c2)
                        for n in range(16):
                            cpar = n % 2
                            cd = tCD[:, cpar * 512:(cpar + 1) * 512]
                            if (sub, j, n - 2) in mk_t1:
                                p.wait("scalar", "dve",
                                       mk_t1[(sub, j, n - 2)])
                            elif j >= 1 and n < 2:
                                p.wait("scalar", "dve",
                                       mk_t1[(sub, j - 1, 14 + n)])
                            elif sub >= 1 and j == 0 and n < 2:
                                p.wait("scalar", "dve",
                                       mk_t1[(sub - 1, 15, 14 + n)])
                            p.op("scalar", "act",
                                 lambda q, j=j, n=n, cd=cd, dd=dd:
                                 q.activation(cd, dd, AF.Exp,
                                              scale=cA[:, j * 16 + n:
                                                       j * 16 + n + 1]))
                            mk_cd = p.n("act")
                            p.wait("vector", "act", mk_cd)
                            p.op("vector", "dve", lambda q, j=j, n=n, cd=cd:
                                 q.tensor_scalar(
                                     tT1[:], cd,
                                     HIN[:, j * 16 + n: j * 16 + n + 1],
                                     None, ALU.mult))
                            mk_t1[(sub, j, n)] = p.n("dve")
                            p.op("vector", "dve", lambda q, n=n:
                                 q.tensor_mul(tT2[:], tT1[:],
                                              tC2[:, n * 512:(n + 1) * 512]))
                            p.op("vector", "dve", lambda q, yh=yh:
                                 q.tensor_add(yh, yh, tT2[:]))
                        p.op("scalar", "act", lambda q, zz=zz, sz=sz:
                             q.activation(sz, zz, AF.Silu))
                        mk_sz = p.n("act")
                        p.op("vector", "dve", lambda q, uu=uu, j=j:
                             q.tensor_scalar(tT1[:], uu, cDS[:, j: j + 1],
                                             None, ALU.mult))
                        p.op("vector", "dve", lambda q, yh=yh:
                             q.tensor_add(tT2[:], tT1[:], yh))
                        p.wait("vector", "act", mk_sz)
                        if (sub, j - 2) in mk_spyp:
                            p.wait("vector", semp("sp", j),
                                   mk_spyp[(sub, j - 2)])
                        elif sub >= 1 and j < 2:
                            p.wait("vector", semp("sp", j),
                                   mk_spyp[(sub - 1, 14 + j)])
                        p.op("vector", "dve", lambda q, par=par, sz=sz:
                             q.tensor_mul(tYP[:, par * 512:(par + 1) * 512],
                                          tT2[:], sz))
                        mk_yp2[(sub, j)] = p.n("dve")
                        p.wait("scalar", "dve", mk_yp2[(sub, j)])
                        p.dma("scalar", semp("sp", j),
                              sp_yp[j, :, t0:t0 + 512],
                              tYP[:, par * 512:(par + 1) * 512])
                        mk_spyp[(sub, j)] = p.n(semp("sp", j))
                p.flush(block)

            with allocs(
                nc.sbuf_tensor("tYPA", [128, 16 * 512], F16),
                nc.sbuf_tensor("tMO", [128, 4 * 2048], F16),
                nc.sbuf_tensor("tX1", [128, 2 * 2048], F16),
                nc.psum_tensor("psQ0", [128, 1024], F32),
                nc.psum_tensor("psQ1", [128, 1024], F32),
                nc.psum_tensor("psQ2", [128, 1024], F32),
                nc.psum_tensor("psQ3", [128, 1024], F32),
            ) as (tYPA, tMO, tX1, psQ0, psQ1, psQ2, psQ3,):
                psQ = [psQ0, psQ1, psQ2, psQ3]
                p.wait("scalar", "sp")
                p.wait("scalar", "sp1")
                for sub in range(NSUB):
                    t0 = sub * TSUB
                    if sub >= 1:
                        p.wait("scalar", "dve")
                        p.wait("scalar", "pe")
                    for j in range(ND):
                        p.dma("scalar", semp("rd", sub),
                              tYPA[:, j * 512:(j + 1) * 512],
                              sp_yp[j, :, t0:t0 + 512])
                    rd_yp = p.n(semp("rd", sub))
                    for d2h in range(2):
                        for jg in range(4):
                            wpar = jg % 2
                            ws = tW[:, wpar * 4096: wpar * 4096 + 4096]
                            if d2h * 4 + jg >= 2:
                                p.wait("gpsimd", "pe")
                            p.dma("gpsimd", semp("wt", d2h * 4 + jg),
                                  ws,
                                  w_outT[:, (d2h * 4 + jg) * 4096:
                                         (d2h * 4 + jg + 1) * 4096])
                            p.wait("tensor", semp("wt", d2h * 4 + jg))
                            p.wait("tensor", semp("rd", sub), rd_yp)
                            if jg == 0:
                                p.wait("tensor", "act")
                            for ji in range(4):
                                j = jg * 4 + ji
                                for it in range(4):
                                    for nh in range(2):
                                        mmfn = (lambda q, j=j, ji=ji, it=it,
                                                nh=nh, ws=ws: q.matmul(
                                                    psQ[it][:, nh * 512:
                                                            (nh + 1) * 512],
                                                    tYPA[:, j * 512 + it * 128:
                                                         j * 512 +
                                                         (it + 1) * 128],
                                                    ws[:, ji * 1024 + nh * 512:
                                                       ji * 1024 +
                                                       (nh + 1) * 512],
                                                    start=(j == 0),
                                                    stop=(j == ND - 1)))
                                        if ji == 3 and it == 3 and nh == 1:
                                            p.op("tensor", "pe", mmfn)
                                        else:
                                            p.opn("tensor", mmfn)
                        p.wait("scalar", "pe")
                        for it in range(4):
                            p.op("scalar", "act", lambda q, it=it, d2h=d2h:
                                 q.copy(tMO[:, it * 2048 + d2h * 1024:
                                            it * 2048 + (d2h + 1) * 1024],
                                        psQ[it][:]))
                    mo_mk = p.n("act")

                    for it in range(4):
                        tt = sub * 4 + it
                        xpar = it % 2
                        xs = tX[:, xpar * 2048:(xpar + 1) * 2048]
                        mo = tMO[:, it * 2048:(it + 1) * 2048]
                        x1 = tX1[:, xpar * 2048:(xpar + 1) * 2048]
                        if it >= 2:
                            p.wait("gpsimd", "dve")
                        p.dma("gpsimd", semp("lx", it), xs,
                              xin[t0 + it * 128: t0 + (it + 1) * 128, :])
                        p.wait("vector", "act", mo_mk)
                        p.op("vector", "dve", lambda q, mo=mo: q.reduce_sum(
                            tSTAT[:, 0:1], mo, axis=mybir.AxisListType.X))
                        p.op("scalar", "act", lambda q, mo=mo: q.activation(
                            tJA[:], mo, AF.Square, scale=cKON[:, 4:5]))
                        p.wait("vector", "act")
                        p.op("vector", "dve", lambda q: q.reduce_sum(
                            tSTAT[:, 1:2], tJA[:], axis=mybir.AxisListType.X))
                        p.op("vector", "dve", lambda q: q.tensor_scalar(
                            tSTAT[:, 2:3], tSTAT[:, 0:1], RD, None, ALU.mult))
                        p.wait("vector", "dve")
                        p.op("vector", "dve", lambda q: q.scalar_tensor_tensor(
                            tSTAT[:, 4:5], tSTAT[:, 2:3], tSTAT[:, 2:3],
                            tSTAT[:, 1:2], ALU.mult, ALU.subtract))
                        p.wait("scalar", "dve")
                        p.op("scalar", "act", lambda q: q.activation(
                            tSTAT[:, 5:6], tSTAT[:, 4:5], AF.Ln,
                    scale=cKON[:, 3:4], bias=cKON[:, 0:1]))
                        p.wait("scalar", "act")
                        p.op("scalar", "act", lambda q: q.activation(
                            tSTAT[:, 6:7], tSTAT[:, 5:6], AF.Exp, scale=cKON[:, 2:3]))
                        p.wait("vector", "act")
                        p.op("vector", "dve", lambda q, mo=mo: q.tensor_scalar(
                            mo, mo, tSTAT[:, 2:3], tSTAT[:, 6:7],
                            ALU.subtract, ALU.mult))
                        p.wait("vector", semp("lx", it))
                        if tt >= 2:
                            p.wait("vector", semp("sp", tt))
                        p.op("vector", "dve", lambda q, x1=x1, xs=xs, mo=mo:
                             q.tensor_add(x1, xs, mo))
                        x1_mk = p.n("dve")
                        p.wait("scalar", "dve", x1_mk)
                        p.dma("scalar", semp("sp", tt), sp_x1[tt, :, :], x1)
                        p.op("vector", "dve", lambda q, x1=x1: q.reduce_sum(
                            tSTAT[:, 0:1], x1, axis=mybir.AxisListType.X))
                        p.op("scalar", "act", lambda q, x1=x1: q.activation(
                            tJA[:], x1, AF.Square, scale=cKON[:, 4:5]))
                        p.wait("vector", "act")
                        p.op("vector", "dve", lambda q: q.reduce_sum(
                            tSTAT[:, 1:2], tJA[:], axis=mybir.AxisListType.X))
                        p.op("vector", "dve", lambda q: q.tensor_scalar(
                            tSTAT[:, 2:3], tSTAT[:, 0:1], RD, None, ALU.mult))
                        p.wait("vector", "dve")
                        p.op("vector", "dve", lambda q: q.scalar_tensor_tensor(
                            tSTAT[:, 4:5], tSTAT[:, 2:3], tSTAT[:, 2:3],
                            tSTAT[:, 1:2], ALU.mult, ALU.subtract))
                        p.wait("scalar", "dve")
                        p.op("scalar", "act", lambda q: q.activation(
                            tSTAT[:, 5:6], tSTAT[:, 4:5], AF.Ln,
                    scale=cKON[:, 3:4], bias=cKON[:, 0:1]))
                        p.wait("scalar", "act")
                        p.op("scalar", "act", lambda q: q.activation(
                            tSTAT[:, 6:7], tSTAT[:, 5:6], AF.Exp, scale=cKON[:, 2:3]))
                        p.wait("vector", "act")
                        p.wait("vector", "tr")
                        p.op("vector", "dve", lambda q, x1=x1:
                             q.tensor_scalar(
                                 tLNTM[:, 0:2048],
                                 x1, tSTAT[:, 2:3], tSTAT[:, 6:7],
                                 ALU.subtract, ALU.mult))
                        p.wait("sync", "dve")
                        for j in range(ND):
                            p.trp(BIGA[:, j * 2048 + tt * 128:
                                       j * 2048 + (tt + 1) * 128],
                                  tLNTM[:, j * 128:(j + 1) * 128])
                p.flush(block)

            with allocs(
                nc.sbuf_tensor("tES", [128, 2 * 2048], F16),
                nc.psum_tensor("psM0", [128, 512], F32),
                nc.psum_tensor("psM1", [128, 512], F32),
                nc.psum_tensor("psM2", [128, 512], F32),
                nc.psum_tensor("psM3", [128, 512], F32),
                nc.psum_tensor("psE0", [128, 512], F32),
                nc.psum_tensor("psE1", [128, 512], F32),
            ) as (tES, psM0, psM1, psM2, psM3, psE0, psE1,):
                _ffn = ExitStack()
                tDO = _ffn.enter_context(
                    nc.sbuf_tensor("tDO", [128, 2 * 4 * 2048], F16))
                tW1 = _ffn.enter_context(
                    nc.sbuf_tensor("tW1", [128, 2 * 8192], F16))
                psM = [psM0, psM1, psM2, psM3]
                psE = [psE0, psE1]
                ea_prev = (0, 0)
                mk_m1grp = {}
                mk_m1ev = {}
                mk_mm2last = {}
                mk_m2grp = {}
                mk_m2ev = {}
                mk_ea = {}
                for oct_ in range(16):
                    dslot = (oct_ % 2) * 8192
                    wslot = (oct_ % 2) * 8192
                    if oct_ >= 2:
                        p.wait("gpsimd", "pe", mk_mm2last[oct_ - 2])
                    p.dma("gpsimd", semp("lx", oct_), tW1[:, wslot:wslot + 8192],
                          w1T[:, oct_ * 8192:(oct_ + 1) * 8192])
                    mk_w1ld = p.n(semp("lx", oct_))
                    for hh in range(4):
                        i = oct_ * 4 + hh
                        h = oct_ * 4 + hh
                        wpar = i % 2
                        ws = tW[:, wpar * 4096: wpar * 4096 + 2048]
                        if i >= 2:
                            p.wait("gpsimd", "pe", mk_m1grp[i - 2])
                        p.dma("gpsimd", semp("wt", i), ws,
                              w0T[:, h * 2048:(h + 1) * 2048])
                        p.wait("tensor", semp("wt", i))
                        if i == 0:
                            p.wait("tensor", "tr")
                        for quad in range(4):
                            pq = psM[quad]
                            if i >= 1:
                                p.wait("tensor", "act",
                                       mk_m1ev[(i - 1, quad)])
                            for k in range(ND):
                                mmfn = (lambda q, k=k, quad=quad, pq=pq, ws=ws:
                                        q.matmul(
                                            pq[:],
                                            ws[:, k * 128:(k + 1) * 128],
                                            BIGA[:, k * 2048 + quad * 512:
                                                 k * 2048 + (quad + 1) * 512],
                                            start=(k == 0), stop=(k == ND - 1)))
                                if k == ND - 1:
                                    p.op("tensor", "pe", mmfn)
                                else:
                                    p.opn("tensor", mmfn)
                            if quad == 3:
                                mk_m1grp[i] = p.n("pe")
                            p.wait("scalar", "pe")
                            if oct_ >= 2 and hh == 0 and quad == 0:
                                p.wait("scalar", "pe", mk_mm2last[oct_ - 2])
                            p.op("scalar", "act",
                                 lambda q, hh=hh, quad=quad, pq=pq, h=h,
                                 dslot=dslot:
                                 q.activation(
                                     tDO[:, dslot + hh * 2048 + quad * 512:
                                         dslot + hh * 2048 + (quad + 1) * 512],
                                     pq[:], AF.Gelu,
                                     bias=cB0[:, h: h + 1]))
                            mk_m1ev[(i, quad)] = p.n("act")
                    mk_do = p.n("act")
                    if oct_ >= 1:
                        p.wait("gpsimd", "ea", ea_prev[0])
                        p.wait("gpsimd", "ea1", ea_prev[1])
                    p.wait("tensor", semp("lx", oct_), mk_w1ld)
                    for d2c in range(16):
                        p.wait("tensor", "act", mk_do)
                        for tq in range(4):
                            idx = d2c * 4 + tq
                            pq = psE[idx % 2]
                            if idx >= 2:
                                p.wait("tensor", "act",
                                       mk_m2ev[(oct_, idx - 2)])
                            elif oct_ >= 1:
                                p.wait("tensor", "act",
                                       mk_m2ev[(oct_ - 1, 62 + idx)])
                            for hh in range(4):
                                mmfn = (lambda q, hh=hh, tq=tq, pq=pq,
                                        dslot=dslot, wslot=wslot, d2c=d2c:
                                        q.matmul(
                                            pq[:],
                                            tW1[:, wslot + hh * 2048 +
                                                d2c * 128:
                                                wslot + hh * 2048 +
                                                (d2c + 1) * 128],
                                            tDO[:, dslot + hh * 2048 + tq * 512:
                                                dslot + hh * 2048 +
                                                (tq + 1) * 512],
                                            start=(hh == 0), stop=(hh == 3)))
                                if hh == 3:
                                    p.op("tensor", "pe", mmfn)
                                else:
                                    p.opn("tensor", mmfn)
                            p.wait("scalar", "pe")
                            eslot = d2c % 2
                            if tq == 0 and (oct_, d2c - 2) in mk_ea:
                                p.wait("scalar", semp("ea", d2c),
                                       mk_ea[(oct_, d2c - 2)])
                            elif tq == 0 and oct_ >= 1 and d2c < 2:
                                p.wait("scalar", semp("ea", d2c),
                                       mk_ea[(oct_ - 1, 14 + d2c)])
                            p.op("scalar", "act",
                                 lambda q, tq=tq, eslot=eslot,
                                 pq2=psE[idx % 2]:
                                 q.copy(tES[:, eslot * 2048 + tq * 512:
                                            eslot * 2048 + (tq + 1) * 512],
                                        pq2[:]))
                            mk_m2ev[(oct_, idx)] = p.n("act")
                        mk_m2grp[(oct_, d2c)] = p.n("pe")
                        mk_es = p.n("act")
                        p.wait("gpsimd", "act", mk_es)
                        p.dma("gpsimd", semp("ea", d2c), sp_E[d2c, :, :],
                              tES[:, (d2c % 2) * 2048:
                                  (d2c % 2 + 1) * 2048],
                              **({} if oct_ == 0 else
                                 dict(accum_op=ALU.add)))
                        mk_ea[(oct_, d2c)] = p.n(semp("ea", d2c))
                    mk_mm2last[oct_] = p.n("pe")
                    ea_prev = (p.n("ea"), p.n("ea1"))

                p.flush(block)
                _ffn.close()

                with allocs(nc.sbuf_tensor("tETM", [128, 4 * 2048], F16), nc.sbuf_tensor("tEQ", [128, 2 * 512], F16), nc.sbuf_tensor("tX1R", [128, 2 * 2048], F16)) as (tETM, tEQ, tX1R,):
                    p.wait("scalar", "ea")
                    p.wait("scalar", "ea1")
                    p.wait("scalar", "sp")
                    p.wait("scalar", "sp1")
                    p.wait("sync", "ea")
                    p.wait("sync", "ea1")
                    for ttg in range(4):
                        if ttg >= 1:
                            p.wait("scalar", "dve")
                        for d2c in range(16):
                            epar = d2c % 2
                            eq = tEQ[:, epar * 512:(epar + 1) * 512]
                            if d2c >= 2:
                                p.wait("scalar", semp("tr", d2c))
                            p.dma("scalar", semp("rd", d2c), eq,
                                  sp_E[d2c, :, ttg * 512:(ttg + 1) * 512])
                            p.wait("vector", semp("rd", d2c))
                            p.op("vector", "dve", lambda q, eq=eq, d2c=d2c:
                                 q.tensor_scalar(eq, eq, cB1[:, d2c: d2c + 1],
                                                 None, ALU.add))
                            p.wait("sync", "dve")
                            for g in range(4):
                                p.trp(tETM[:, g * 2048 + d2c * 128:
                                           g * 2048 + (d2c + 1) * 128],
                                      eq[:, g * 128:(g + 1) * 128],
                                      sem=semp("tr", d2c))
                        p.wait("vector", "tr")
                        p.wait("vector", "tr1")
                        p.wait("scalar", "tr")
                        p.wait("scalar", "tr1")
                        for g in range(4):
                            tt = ttg * 4 + g
                            em = tETM[:, g * 2048:(g + 1) * 2048]
                            xpar = g % 2
                            x1r = tX1R[:, xpar * 2048:(xpar + 1) * 2048]
                            if g >= 2:
                                p.wait("scalar", "dve")
                            p.dma("scalar", semp("xr", g), x1r,
                                  sp_x1[tt, :, :])
                            p.op("vector", "dve", lambda q, em=em:
                                 q.reduce_sum(tSTAT[:, 0:1], em,
                                              axis=mybir.AxisListType.X))
                            p.op("scalar", "act", lambda q, em=em:
                                 q.activation(tJA[:], em, AF.Square, scale=cKON[:, 4:5]))
                            p.wait("vector", "act")
                            p.op("vector", "dve", lambda q: q.reduce_sum(
                                tSTAT[:, 1:2], tJA[:],
                                axis=mybir.AxisListType.X))
                            p.op("vector", "dve", lambda q: q.tensor_scalar(
                                tSTAT[:, 2:3], tSTAT[:, 0:1], RD, None,
                                ALU.mult))
                            p.wait("vector", "dve")
                            p.op("vector", "dve", lambda q: q.scalar_tensor_tensor(
                                tSTAT[:, 4:5], tSTAT[:, 2:3], tSTAT[:, 2:3],
                                tSTAT[:, 1:2], ALU.mult, ALU.subtract))
                            p.wait("scalar", "dve")
                            p.op("scalar", "act", lambda q: q.activation(
                                tSTAT[:, 5:6], tSTAT[:, 4:5], AF.Ln,
                    scale=cKON[:, 3:4], bias=cKON[:, 0:1]))
                            p.wait("scalar", "act")
                            p.op("scalar", "act", lambda q: q.activation(
                                tSTAT[:, 6:7], tSTAT[:, 5:6], AF.Exp, scale=cKON[:, 2:3]))
                            p.wait("vector", "act")
                            p.op("vector", "dve", lambda q, em=em:
                                 q.tensor_scalar(em, em, tSTAT[:, 2:3],
                                                 tSTAT[:, 6:7],
                                                 ALU.subtract, ALU.mult))
                            p.wait("vector", semp("xr", g))
                            if tt >= 2:
                                p.wait("vector", "ou", 16 * (tt - 1))
                            osl = tOUT[:, (tt % 2) * 2048:
                                       (tt % 2 + 1) * 2048]
                            p.op("vector", "dve", lambda q, em=em, x1r=x1r,
                                 osl=osl:
                                 q.tensor_add(osl, x1r, em))
                            p.wait("sync", "dve")
                            p.dma("sync", "ou",
                                  out[tt * 128:(tt + 1) * 128, :], osl)
                    p.wait("sync", "ou")
                    p.flush(block)

    return nc


_CACHED = {}


def _prep_inputs(inputs):
    x = np.ascontiguousarray(
        np.asarray(inputs["L0_attn_x"], dtype=np.float32).reshape(16384, D))
    W_in = np.asarray(inputs["L0_attn_W_in"], dtype=np.float32)
    conv_w = np.asarray(inputs["L0_attn_conv_w"], dtype=np.float32)
    conv_b = np.asarray(inputs["L0_attn_conv_b"], dtype=np.float32)
    W_xp = np.asarray(inputs["L0_attn_W_xproj"], dtype=np.float32)
    W_dt = np.asarray(inputs["L0_attn_W_dt"], dtype=np.float32)
    b_dt = np.asarray(inputs["L0_attn_b_dt"], dtype=np.float32)
    A_log = np.asarray(inputs["L0_attn_A_log"], dtype=np.float32)
    Dskip = np.asarray(inputs["L0_attn_Dskip"], dtype=np.float32)
    W_out = np.asarray(inputs["L0_attn_W_out"], dtype=np.float32)
    W0 = np.asarray(inputs["L0_ffn_W0"], dtype=np.float32)
    b0 = np.asarray(inputs["L0_ffn_bias0"], dtype=np.float32)
    W1 = np.asarray(inputs["L0_ffn_W1"], dtype=np.float32)
    b1 = np.asarray(inputs["L0_ffn_bias1"], dtype=np.float32)

    shared = {
        "w_inT": np.ascontiguousarray(
            W_in.T.reshape(16, 128, 32, 128).transpose(1, 2, 0, 3)
            .reshape(128, 32 * 2048)),
        "w_xpT": np.ascontiguousarray(
            W_xp.T.reshape(16, 128, 160).transpose(1, 0, 2).reshape(128, 16 * 160)),
        "w_dtT": np.ascontiguousarray(W_dt.T),
        "w_outT": np.ascontiguousarray(
            W_out.T.reshape(4, 4, 128, 2, 1024).transpose(2, 3, 0, 1, 4)
            .reshape(128, 32 * 1024)),
        "w0T": np.ascontiguousarray(
            W0.T.reshape(16, 128, 64, 128).transpose(1, 2, 0, 3)
            .reshape(128, 64 * 2048)),
        "w1T": np.ascontiguousarray(
            W1.T.reshape(16, 4, 128, 2048).transpose(2, 0, 1, 3)
            .reshape(128, 16 * 8192)),
        "c_alog": np.ascontiguousarray(
            A_log.reshape(16, 128, 16).transpose(1, 0, 2).reshape(128, 256)),
        "c_cw": np.ascontiguousarray(
            conv_w.reshape(16, 128, 4).transpose(1, 0, 2).reshape(128, 64)),
        "c_cb": np.ascontiguousarray(conv_b.reshape(16, 128).T),
        "c_bdt": np.ascontiguousarray(b_dt.reshape(16, 128).T),
        "c_ds": np.ascontiguousarray(Dskip.reshape(16, 128).T),
        "c_b0": np.ascontiguousarray(b0.reshape(64, 128).T),
        "c_b1": np.ascontiguousarray(b1.reshape(16, 128).T),
    }

    in_maps = []
    for c in range(NCORE):
        m = dict(shared)
        m["xin"] = np.ascontiguousarray(x[c * TC:(c + 1) * TC])
        halo = np.zeros((16, D), dtype=np.float32)
        if c > 0:
            halo[13:16] = x[c * TC - 3: c * TC]
        m["xhalo"] = halo
        csel = np.zeros((128, 8), dtype=np.float32)
        if c > 0:
            csel[:, c - 1] = 1.0
        m["c_csel"] = csel
        in_maps.append(m)
    return in_maps


def kernel(**inputs):
    if "nc" not in _CACHED:
        _CACHED["nc"] = build_nc()
    nc = _CACHED["nc"]
    in_maps = _prep_inputs(inputs)
    res = run_bass_kernel_spmd(nc, in_maps, core_ids=list(range(NCORE)))
    outp = np.concatenate([res.results[c]["out"] for c in range(NCORE)], axis=0)
    return outp.reshape(1, 16384, D).astype(np.float32)


# version 14 (final)
# speedup vs baseline: 1.0604x; 1.0604x over previous
import numpy as np
from contextlib import ExitStack, contextmanager
import concourse.bass as bass
import concourse.mybir as mybir
from concourse.bass_utils import run_bass_kernel_spmd


@contextmanager
def allocs(*cms):
    with ExitStack() as st:
        yield tuple(st.enter_context(cm) for cm in cms)

F32 = mybir.dt.float32
F16 = mybir.dt.float16
AF = mybir.ActivationFunctionType
ALU = mybir.AluOpType

NCORE = 8
TC = 2048
D = 2048
FF = 4096
H = 8192
ND = 16
NTT = 16
NSUB = 4
TSUB = 512
EPS = 1e-5
RD = 1.0 / 2048.0
UW = 2051


class Plan:
    ENGINES = ("sync", "scalar", "vector", "tensor", "gpsimd")

    def __init__(self, sems):
        self.ops = {e: [] for e in self.ENGINES}
        self.cnt = {}
        self.S = sems

    def emit(self, eng, fn):
        self.ops[eng].append(fn)

    def wait(self, eng, sem, val=None):
        v = self.cnt.get(sem, 0) if val is None else val
        if v <= 0:
            return
        self.emit(eng, lambda q, s=self.S[sem], vv=v: q.wait_ge(s, vv))

    def dma(self, eng, sem, outap, inap, **kw):
        self.cnt[sem] = self.cnt.get(sem, 0) + 16
        self.emit(eng, lambda q, o=outap, i=inap, s=self.S[sem], k=kw:
                  q.dma_start(out=o, in_=i, **k).then_inc(s, 16))

    def trp(self, outap, inap, sem="tr"):
        self.cnt[sem] = self.cnt.get(sem, 0) + 16
        self.emit("sync", lambda q, o=outap, i=inap, s=self.S[sem]:
                  q.dma_start(out=o, in_=i, transpose=True).then_inc(s, 16))

    def op(self, eng, sem, fn):
        self.cnt[sem] = self.cnt.get(sem, 0) + 1
        self.emit(eng, lambda q, f=fn, s=self.S[sem]: f(q).then_inc(s, 1))

    def opn(self, eng, fn):
        self.emit(eng, fn)

    def n(self, sem):
        return self.cnt.get(sem, 0)

    def flush(self, block):
        plans = self.ops
        self.ops = {e: [] for e in self.ENGINES}

        @block.sync
        def _(q):
            for f in plans["sync"]:
                f(q)

        @block.scalar
        def _(q):
            for f in plans["scalar"]:
                f(q)

        @block.vector
        def _(q):
            for f in plans["vector"]:
                f(q)

        @block.tensor
        def _(q):
            for f in plans["tensor"]:
                f(q)

        @block.gpsimd
        def _(q):
            for f in plans["gpsimd"]:
                f(q)


def build_nc():
    nc = bass.Bass("TRN2", target_bir_lowering=False, debug=False)

    xin = nc.dram_tensor("xin", [TC, D], F32, kind="ExternalInput").ap()
    xhalo = nc.dram_tensor("xhalo", [16, D], F32, kind="ExternalInput").ap()
    w_inT = nc.dram_tensor("w_inT", [128, 32 * 2048], F32, kind="ExternalInput").ap()
    w_xpT = nc.dram_tensor("w_xpT", [128, 16 * 160], F32, kind="ExternalInput").ap()
    w_dtT = nc.dram_tensor("w_dtT", [128, D], F32, kind="ExternalInput").ap()
    w_outT = nc.dram_tensor("w_outT", [128, 32 * 1024], F32, kind="ExternalInput").ap()
    w0T = nc.dram_tensor("w0T", [128, 64 * 2048], F32, kind="ExternalInput").ap()
    w1T = nc.dram_tensor("w1T", [128, 16 * 8192], F32, kind="ExternalInput").ap()
    c_alog = nc.dram_tensor("c_alog", [128, 256], F32, kind="ExternalInput").ap()
    c_cw = nc.dram_tensor("c_cw", [128, 64], F32, kind="ExternalInput").ap()
    c_cb = nc.dram_tensor("c_cb", [128, 16], F32, kind="ExternalInput").ap()
    c_bdt = nc.dram_tensor("c_bdt", [128, 16], F32, kind="ExternalInput").ap()
    c_ds = nc.dram_tensor("c_ds", [128, 16], F32, kind="ExternalInput").ap()
    c_b0 = nc.dram_tensor("c_b0", [128, 64], F32, kind="ExternalInput").ap()
    c_b1 = nc.dram_tensor("c_b1", [128, 16], F32, kind="ExternalInput").ap()
    c_csel = nc.dram_tensor("c_csel", [128, 8], F32, kind="ExternalInput").ap()
    out = nc.dram_tensor("out", [TC, D], F32, kind="ExternalOutput").ap()

    sp_u = nc.dram_tensor("sp_u", [ND, 128, TC], F16).ap()
    sp_z = nc.dram_tensor("sp_z", [ND, 128, TC], F16).ap()
    sp_dt = nc.dram_tensor("sp_dt", [ND, 128, TC], F16).ap()
    sp_yh = nc.dram_tensor("sp_yh", [ND, 128, TC], F16).ap()
    sp_D = nc.dram_tensor("sp_D", [ND, 128, TC], F32).ap()
    sp_C = nc.dram_tensor("sp_C", [16, TC], F16).ap()
    sp_yp = nc.dram_tensor("sp_yp", [ND, 128, TC], F16).ap()
    sp_x1 = nc.dram_tensor("sp_x1", [NTT, 128, D], F16).ap()
    sp_E = nc.dram_tensor("sp_E", [16, 128, TC], F16).ap()
    cc_in = nc.dram_tensor("cc_in", [D, 32], F32).ap()
    cc_out = nc.dram_tensor("cc_out", [NCORE * D, 32], F32).ap()

    with allocs(
        nc.semaphore("s_lc"),
        nc.semaphore("s_lx"),
        nc.semaphore("s_tr"),
        nc.semaphore("s_wt"),
        nc.semaphore("s_sp"),
        nc.semaphore("s_rd"),
        nc.semaphore("s_ea"),
        nc.semaphore("s_ou"),
        nc.semaphore("s_cc"),
        nc.semaphore("s_pe"),
        nc.semaphore("s_act"),
        nc.semaphore("s_dve"),
        nc.semaphore("s_gpc"),
        nc.semaphore("s_lx1"),
        nc.semaphore("s_wt1"),
        nc.semaphore("s_sp1"),
        nc.semaphore("s_rd1"),
        nc.semaphore("s_ea1"),
        nc.semaphore("s_tr1"),
        nc.semaphore("s_xr"),
        nc.semaphore("s_xr1"),
    ) as (s_lc, s_lx, s_tr, s_wt, s_sp, s_rd, s_ea, s_ou, s_cc, s_pe, s_act, s_dve, s_gpc, s_lx1, s_wt1, s_sp1, s_rd1, s_ea1, s_tr1, s_xr, s_xr1,):
        S = dict(lc=s_lc, lx=s_lx, tr=s_tr, wt=s_wt, sp=s_sp, rd=s_rd,
                 ea=s_ea, ou=s_ou, cc=s_cc, pe=s_pe, act=s_act, dve=s_dve,
                 gpc=s_gpc, lx1=s_lx1, wt1=s_wt1, sp1=s_sp1, rd1=s_rd1,
                 ea1=s_ea1, tr1=s_tr1, xr=s_xr, xr1=s_xr1)

        def semp(base, par):
            return base if par % 2 == 0 else base + "1"
        p = Plan(S)

        with allocs(
            nc.sbuf_tensor("cA", [128, 256], F32),
            nc.sbuf_tensor("cCW", [128, 64], F32),
            nc.sbuf_tensor("cCB", [128, 16], F32),
            nc.sbuf_tensor("cBD", [128, 16], F32),
            nc.sbuf_tensor("cDS", [128, 16], F32),
            nc.sbuf_tensor("cB0", [128, 64], F32),
            nc.sbuf_tensor("cB1", [128, 16], F32),
            nc.sbuf_tensor("cSELQ", [128, 8], F32),
            nc.sbuf_tensor("tSEL", [16, 2048], F16),
            nc.sbuf_tensor("tJA", [128, 2048], F16),
            nc.sbuf_tensor("tX", [128, 2 * 2048], F16),
            nc.sbuf_tensor("tLNTM", [128, 2048], F16),
            nc.sbuf_tensor("tLNHD", [128, 2048], F16),
            nc.sbuf_tensor("tW", [128, 2 * 4096], F16),
            nc.sbuf_tensor("tOUT", [128, 2 * 2048], F32),
            nc.sbuf_tensor("tSTAT", [128, 8], F32),
            nc.sbuf_tensor("cKON", [128, 6], F32),
            nc.sbuf_tensor("carry", [128, 256], F32),
            nc.sbuf_tensor("dcarry", [128, 16], F32),
            nc.sbuf_tensor("hinA", [128, 256], F32),
            nc.sbuf_tensor("hinB", [128, 256], F32),
            nc.sbuf_tensor("tBB", [16, 2048], F16),
            nc.sbuf_tensor("tCCt", [16, 2048], F16),
            nc.sbuf_tensor("BIGA", [128, 16 * 2048], F16),
            nc.Block(),
        ) as (cA, cCW, cCB, cBD, cDS, cB0, cB1, cSELQ, tSEL, tJA, tX, tLNTM, tLNHD, tW, tOUT, tSTAT, cKON, carry, dcarry, hinA, hinB, tBB, tCCt, BIGA, block,):
            _pro = ExitStack()
            tXH = _pro.enter_context(nc.sbuf_tensor("tXH", [16, 2048], F32))
            tLNH = _pro.enter_context(nc.sbuf_tensor("tLNH", [128, 2048], F16))
            tIOT = _pro.enter_context(nc.sbuf_tensor("tIOT", [16, 2048], F32))
            cAe = _pro.enter_context(nc.sbuf_tensor("cAe", [128, 256], F32))
            if True:
                for dst, src in [(cAe, c_alog), (cCW, c_cw), (cCB, c_cb),
                                 (cBD, c_bdt), (cDS, c_ds), (cB0, c_b0),
                                 (cB1, c_b1), (cSELQ, c_csel)]:
                    p.dma("sync", "lc", dst[:], src[:, :])
                p.dma("sync", "lc", tXH[:], xhalo[:, :])
                p.op("gpsimd", "gpc", lambda q: q.iota(
                    tIOT[:], pattern=[[1, 16], [0, 128]], base=0,
                    channel_multiplier=-1,
                    allow_small_or_imprecise_dtypes=True))
                p.wait("vector", "gpc")
                p.op("vector", "dve", lambda q: q.tensor_scalar(
                    tSEL[:], tIOT[:], 0.0, None, ALU.is_equal))
                p.op("vector", "dve", lambda q: q.memset(cKON[:, 0:1], float(EPS)))
                p.op("vector", "dve", lambda q: q.memset(cKON[:, 1:2], 1.0))
                p.op("vector", "dve", lambda q: q.memset(cKON[:, 2:3], -0.5))
                p.op("vector", "dve", lambda q: q.memset(cKON[:, 3:4], -1.0))
                p.op("vector", "dve", lambda q: q.memset(cKON[:, 4:5], float(2048.0 ** -0.5)))
                for t in (carry, dcarry, hinA, hinB):
                    p.op("vector", "dve", lambda q, t=t: q.memset(t[:], 0.0))
                p.wait("scalar", "lc")
                p.op("scalar", "act", lambda q: q.activation(cA[:], cAe[:], AF.Exp))
                p.wait("vector", "act")
                p.op("vector", "dve", lambda q: q.tensor_scalar(
                    cA[:], cA[:], -1.0, None, ALU.mult))
                p.wait("vector", "lc")
                p.wait("scalar", "dve")
                p.op("vector", "dve", lambda q: q.reduce_sum(
                    tSTAT[0:16, 0:1], tXH[:], axis=mybir.AxisListType.X))
                p.op("scalar", "act", lambda q: q.activation(
                    tJA[0:16, :], tXH[:], AF.Square, scale=cKON[0:16, 4:5]))
                p.wait("vector", "act")
                p.op("vector", "dve", lambda q: q.reduce_sum(
                    tSTAT[0:16, 1:2], tJA[0:16, :], axis=mybir.AxisListType.X))
                p.op("vector", "dve", lambda q: q.tensor_scalar(
                    tSTAT[0:16, 2:3], tSTAT[0:16, 0:1], RD, None, ALU.mult))
                p.wait("vector", "dve")
                p.op("vector", "dve", lambda q: q.scalar_tensor_tensor(
                    tSTAT[0:16, 4:5], tSTAT[0:16, 2:3], tSTAT[0:16, 2:3],
                    tSTAT[0:16, 1:2], ALU.mult, ALU.subtract))
                p.wait("scalar", "dve")
                p.op("scalar", "act", lambda q: q.activation(
                    tSTAT[0:16, 5:6], tSTAT[0:16, 4:5], AF.Ln,
                    scale=cKON[0:16, 3:4], bias=cKON[0:16, 0:1]))
                p.wait("scalar", "act")
                p.op("scalar", "act", lambda q: q.activation(
                    tSTAT[0:16, 6:7], tSTAT[0:16, 5:6], AF.Exp, scale=cKON[0:16, 2:3]))
                p.wait("vector", "act")
                p.opn("vector", lambda q: q.memset(tLNH[:], 0.0))
                p.op("vector", "dve", lambda q: q.tensor_scalar(
                    tLNH[0:16, :], tXH[:], tSTAT[0:16, 2:3], tSTAT[0:16, 6:7],
                    ALU.subtract, ALU.mult))
                p.wait("sync", "dve")
                for j in range(ND):
                    p.trp(tLNHD[:, j * 128:(j + 1) * 128],
                          tLNH[:, j * 128:(j + 1) * 128])

                mk_tr_tile = {}
                for i in range(NTT):
                    par = i % 2
                    xs = tX[:, par * 2048:(par + 1) * 2048]
                    if i >= 2:
                        p.wait("gpsimd", "dve")
                    p.dma("gpsimd", semp("lx", i), xs,
                          xin[i * 128:(i + 1) * 128, :])
                    p.wait("vector", semp("lx", i))
                    p.wait("scalar", semp("lx", i))
                    p.op("vector", "dve", lambda q, xs=xs: q.reduce_sum(
                        tSTAT[:, 0:1], xs, axis=mybir.AxisListType.X))
                    p.op("scalar", "act", lambda q, xs=xs: q.activation(
                        tJA[:], xs, AF.Square, scale=cKON[:, 4:5]))
                    p.wait("vector", "act")
                    p.op("vector", "dve", lambda q: q.reduce_sum(
                        tSTAT[:, 1:2], tJA[:], axis=mybir.AxisListType.X))
                    p.op("vector", "dve", lambda q: q.tensor_scalar(
                        tSTAT[:, 2:3], tSTAT[:, 0:1], RD, None, ALU.mult))
                    p.wait("vector", "dve")
                    p.op("vector", "dve", lambda q: q.scalar_tensor_tensor(
                        tSTAT[:, 4:5], tSTAT[:, 2:3], tSTAT[:, 2:3],
                        tSTAT[:, 1:2], ALU.mult, ALU.subtract))
                    p.wait("scalar", "dve")
                    p.op("scalar", "act", lambda q: q.activation(
                        tSTAT[:, 5:6], tSTAT[:, 4:5], AF.Ln,
                    scale=cKON[:, 3:4], bias=cKON[:, 0:1]))
                    p.wait("scalar", "act")
                    p.op("scalar", "act", lambda q: q.activation(
                        tSTAT[:, 6:7], tSTAT[:, 5:6], AF.Exp, scale=cKON[:, 2:3]))
                    p.wait("vector", "act")
                    if i >= 1:
                        p.wait("vector", "tr")
                    p.op("vector", "dve", lambda q, xs=xs:
                         q.tensor_scalar(
                             tLNTM[:, 0:2048], xs,
                             tSTAT[:, 2:3], tSTAT[:, 6:7],
                             ALU.subtract, ALU.mult))
                    p.wait("sync", "dve")
                    for j in range(ND):
                        p.trp(BIGA[:, j * 2048 + i * 128:
                                   j * 2048 + (i + 1) * 128],
                              tLNTM[:, j * 128:(j + 1) * 128])
                    mk_tr_tile[i] = p.n("tr")
                p.flush(block)
                _pro.close()

                with allocs(
                    nc.sbuf_tensor("tDTR", [128, 2048], F16),
                    nc.psum_tensor("psW", [128, 2048], F32),
                    nc.psum_tensor("psA", [128, 512], F32),
                    nc.psum_tensor("psB", [128, 512], F32),
                    nc.psum_tensor("psH", [128, 16], F32),
                ) as (tDTR, psW, psA, psB, psH,):
                    _bd = ExitStack()
                    tU = _bd.enter_context(
                        nc.sbuf_tensor("tU", [128, 16 * UW], F16))
                    tZS = _bd.enter_context(
                        nc.sbuf_tensor("tZS", [128, 2048], F16))
                    evmk = {}
                    pemk = {}
                    qev = {}
                    qpe = {}
                    for f in range(32):
                        par = f % 2
                        ws = tW[:, par * 4096: par * 4096 + 2048]
                        if f >= 2:
                            p.wait("gpsimd", "pe", pemk[f - 2])
                        p.dma("gpsimd", semp("wt", f), ws,
                              w_inT[:, f * 2048:(f + 1) * 2048])
                        p.wait("tensor", semp("wt", f))
                        for quad in range(4):
                            if f == 0:
                                p.wait("tensor", "tr",
                                       mk_tr_tile[quad * 4 + 3])
                            if f >= 1:
                                p.wait("tensor", "act", qev[(f - 1, quad)])
                            for k in range(ND):
                                mmfn = (lambda q, k=k, quad=quad, ws=ws:
                                        q.matmul(
                                            psW[:, quad * 512:(quad + 1) * 512],
                                            ws[:, k * 128:(k + 1) * 128],
                                            BIGA[:, k * 2048 + quad * 512:
                                                 k * 2048 + (quad + 1) * 512],
                                            start=(k == 0), stop=(k == ND - 1)))
                                if k == ND - 1:
                                    p.op("tensor", "pe", mmfn)
                                else:
                                    p.opn("tensor", mmfn)
                            qpe[(f, quad)] = p.n("pe")
                        if f < ND:
                            if f >= 1:
                                p.wait("tensor", "act", qev[(f - 1, 4)])
                            for k in range(ND):
                                mmfn = (lambda q, k=k, ws=ws: q.matmul(
                                    psH[:], ws[:, k * 128:(k + 1) * 128],
                                    tLNHD[:, k * 128: k * 128 + 16],
                                    start=(k == 0), stop=(k == ND - 1)))
                                if k == ND - 1:
                                    p.op("tensor", "pe", mmfn)
                                else:
                                    p.opn("tensor", mmfn)
                        pemk[f] = p.n("pe")
                        if f < ND:
                            for quad in range(4):
                                p.wait("scalar", "pe", qpe[(f, quad)])
                                p.op("scalar", "act",
                                     lambda q, f=f, quad=quad: q.copy(
                                         tU[:, f * UW + 3 + quad * 512:
                                            f * UW + 3 + (quad + 1) * 512],
                                         psW[:, quad * 512:(quad + 1) * 512]))
                                qev[(f, quad)] = p.n("act")
                            p.wait("scalar", "pe", pemk[f])
                            p.op("scalar", "act", lambda q, f=f: q.copy(
                                tU[:, f * UW: f * UW + 3], psH[:, 13:16]))
                            qev[(f, 4)] = p.n("act")
                        else:
                            p.wait("scalar", "pe", pemk[f])
                            if f >= 18:
                                p.wait("scalar", semp("sp", f))
                            p.op("scalar", "act", lambda q: q.copy(
                                tZS[:], psW[:]))
                            for quad in range(4):
                                qev[(f, quad)] = p.n("act")
                            p.wait("scalar", "act")
                            p.dma("scalar", semp("sp", f), sp_z[f - 16, :, :],
                                  tZS[:])
                        evmk[f] = p.n("act")

                    for j in range(ND):
                        u0 = tU[:, j * UW + 0: j * UW + 2048]
                        u1 = tU[:, j * UW + 1: j * UW + 2049]
                        u2 = tU[:, j * UW + 2: j * UW + 2050]
                        u3 = tU[:, j * UW + 3: j * UW + 2051]
                        ca = tX[:, 0:2048]
                        cb = tX[:, 2048:4096]
                        p.wait("vector", "act", evmk[j])
                        if j >= 1:
                            p.wait("vector", "act")
                        p.op("vector", "dve", lambda q, u0=u0, ca=ca, j=j:
                             q.tensor_scalar(ca, u0,
                                             cCW[:, j * 4: j * 4 + 1],
                                             None, ALU.mult))
                        p.op("vector", "dve", lambda q, u1=u1, ca=ca, cb=cb, j=j:
                             q.scalar_tensor_tensor(
                                 cb, u1, cCW[:, j * 4 + 1: j * 4 + 2], ca,
                                 ALU.mult, ALU.add))
                        p.op("vector", "dve", lambda q, u2=u2, ca=ca, cb=cb, j=j:
                             q.scalar_tensor_tensor(
                                 ca, u2, cCW[:, j * 4 + 2: j * 4 + 3], cb,
                                 ALU.mult, ALU.add))
                        p.op("vector", "dve", lambda q, u3=u3, ca=ca, cb=cb, j=j:
                             q.scalar_tensor_tensor(
                                 cb, u3, cCW[:, j * 4 + 3: j * 4 + 4], ca,
                                 ALU.mult, ALU.add))
                        p.wait("scalar", "dve")
                        p.op("scalar", "act", lambda q, u3=u3, cb=cb, j=j:
                             q.activation(u3, cb, AF.Silu,
                                          bias=cCB[:, j: j + 1]))
                        p.wait("scalar", "act")
                        p.dma("scalar", semp("sp", j), sp_u[j, :, :], u3)
                    uconv_mk = p.n("act")

                    p.wait("gpsimd", "pe", pemk[31])
                    p.dma("gpsimd", "wt", tW[:, 0:2560], w_xpT[:, :])
                    wxp = tW[:, 0:2560]
                    p.wait("tensor", "wt")
                    p.wait("tensor", "wt1")
                    p.wait("tensor", "act", uconv_mk)
                    for quad in range(4):
                        for k in range(ND):
                            mmfn = (lambda q, k=k, quad=quad: q.matmul(
                                psW[:, quad * 512:(quad + 1) * 512],
                                wxp[:, k * 160: k * 160 + 128],
                                tU[:, k * UW + 3 + quad * 512:
                                   k * UW + 3 + (quad + 1) * 512],
                                start=(k == 0), stop=(k == ND - 1)))
                            if quad == 3 and k == ND - 1:
                                p.op("tensor", "pe", mmfn)
                            else:
                                p.opn("tensor", mmfn)
                    p.wait("scalar", "pe")
                    p.op("scalar", "act", lambda q: q.copy(tDTR[:], psW[:]))
                    for tgt, coff in ((tBB, 128), (tCCt, 144)):
                        for quad in range(4):
                            ps = psA if quad % 2 == 0 else psB
                            if not (tgt is tBB and quad == 0):
                                p.wait("tensor", "act")
                            for k in range(ND):
                                mmfn = (lambda q, k=k, quad=quad, ps=ps,
                                        coff=coff:
                                        q.matmul(
                                            ps[0:16, :],
                                            wxp[:, k * 160 + coff:
                                                k * 160 + coff + 16],
                                            tU[:, k * UW + 3 + quad * 512:
                                               k * UW + 3 + (quad + 1) * 512],
                                            start=(k == 0), stop=(k == ND - 1)))
                                if k == ND - 1:
                                    p.op("tensor", "pe", mmfn)
                                else:
                                    p.opn("tensor", mmfn)
                            p.wait("scalar", "pe")
                            p.op("scalar", "act",
                                 lambda q, quad=quad, ps=ps, tgt=tgt: q.copy(
                                     tgt[:, quad * 512:(quad + 1) * 512],
                                     ps[0:16, :]))
                    p.wait("scalar", "act")
                    p.dma("scalar", "sp", sp_C[:, :], tCCt[:])
                    p.wait("vector", "pe")
                    p.flush(block)
                    _bd.close()

                    with allocs(nc.sbuf_tensor("tDTw", [128, 2 * 2048], F16), nc.sbuf_tensor("tSP1", [128, 2048], F16), nc.sbuf_tensor("tDF", [128, 2048], F32), nc.sbuf_tensor("tONED2", [128, 2048], F16)) as (tDTw, tSP1, tDF, tONED2,):
                        p.opn("vector", lambda q: q.memset(tONED2[:], 1.0))
                        p.wait("scalar", "sp")
                        p.wait("vector", "sp")
                        p.dma("gpsimd", "wt1", tW[:, 4096:4096 + 2048],
                              w_dtT[:, :])
                        wdt = tW[:, 4096:4096 + 2048]
                        p.wait("tensor", "wt1")
                        for j in range(ND):
                            par = j % 2
                            dts = tDTw[:, par * 2048:(par + 1) * 2048]
                            p.wait("tensor", "act")
                            for quad in range(4):
                                mmfn = (lambda q, j=j, quad=quad:
                                        q.matmul(
                                            psW[:, quad * 512:(quad + 1) * 512],
                                            wdt[:, j * 128:(j + 1) * 128],
                                            tDTR[:, quad * 512:(quad + 1) * 512],
                                            start=True, stop=True))
                                if quad == 3:
                                    p.op("tensor", "pe", mmfn)
                                else:
                                    p.opn("tensor", mmfn)
                            p.wait("scalar", "pe")
                            if j >= 2:
                                p.wait("scalar", semp("sp", j))
                            p.op("scalar", "act", lambda q, j=j: q.activation(
                                tSP1[:], psW[:], AF.Exp,
                                bias=cBD[:, j: j + 1]))
                            p.op("scalar", "act", lambda q, dts=dts:
                                 q.activation(dts, tSP1[:], AF.Ln, bias=cKON[:, 1:2]))
                            mk_dt = p.n("act")
                            p.wait("scalar", "act")
                            p.dma("scalar", semp("sp", j), sp_dt[j, :, :], dts)
                            p.wait("vector", "act", mk_dt)
                            if j >= 1:
                                p.wait("vector", semp("sp", j - 1))
                            p.op("vector", "dve", lambda q, dts=dts:
                                 q.tensor_tensor_scan(
                                     tDF[:], tONED2[:], dts, 0.0,
                                     ALU.mult, ALU.add))
                            p.op("vector", "dve", lambda q, j=j: q.tensor_copy(
                                dcarry[:, j: j + 1], tDF[:, 2047:2048]))
                            p.wait("scalar", "dve")
                            p.dma("scalar", semp("sp", j), sp_D[j, :, :], tDF[:])
                        p.flush(block)

                with allocs(
                    nc.sbuf_tensor("tBC", [128, 32 * 512], F16),
                    nc.sbuf_tensor("tUS", [128, 2 * 512], F16),
                    nc.sbuf_tensor("tDTs", [128, 2 * 512], F16),
                    nc.sbuf_tensor("tDTU", [128, 512], F16),
                    nc.sbuf_tensor("tDA", [128, 2 * 512], F16),
                    nc.sbuf_tensor("tDBU", [128, 2 * 512], F16),
                    nc.sbuf_tensor("tHH", [128, 2 * 512], F16),
                    nc.sbuf_tensor("tHC", [128, 512], F16),
                    nc.sbuf_tensor("tYH", [128, 2 * 512], F16),
                ) as (tBC, tUS, tDTs, tDTU, tDA, tDBU, tHH, tHC, tYH,):
                    p.wait("scalar", "dve")
                    p.wait("scalar", "sp")
                    p.wait("scalar", "sp1")
                    mk_dA = {}
                    mk_dtu = {}
                    mk_scan = {}
                    mk_spyh = {}
                    for sub in range(NSUB):
                        t0 = sub * TSUB
                        if sub >= 1:
                            p.wait("scalar", "dve")
                        for half, src in ((0, tBB), (1, tCCt)):
                            for n in range(16):
                                ps = psA if n % 2 == 0 else psB
                                p.wait("tensor", "act")
                                p.op("tensor", "pe",
                                     lambda q, n=n, t0=t0, ps=ps, src=src:
                                     q.matmul(
                                         ps[:],
                                         tSEL[:, n * 128:(n + 1) * 128],
                                         src[:, t0:t0 + 512],
                                         start=True, stop=True))
                                p.wait("scalar", "pe")
                                p.op("scalar", "act",
                                     lambda q, n=n, half=half, ps=ps: q.copy(
                                         tBC[:, (half * 16 + n) * 512:
                                             (half * 16 + n + 1) * 512],
                                         ps[:]))
                        mk_bc = p.n("act")

                        for j in range(ND):
                            upar = j % 2
                            us = tUS[:, upar * 512:(upar + 1) * 512]
                            dts = tDTs[:, upar * 512:(upar + 1) * 512]
                            if j >= 2:
                                p.wait("scalar", "act", mk_dA[(sub, j - 2, 15)])
                                p.wait("scalar", "dve", mk_dtu[(sub, j - 2)])
                            p.dma("scalar", semp("rd", j), us,
                                  sp_u[j, :, t0:t0 + 512])
                            p.dma("scalar", semp("rd", j), dts,
                                  sp_dt[j, :, t0:t0 + 512])
                            p.wait("vector", semp("rd", j))
                            p.wait("vector", "act", mk_bc)
                            if (sub, j - 2) in mk_spyh:
                                p.wait("vector", semp("sp", j),
                                       mk_spyh[(sub, j - 2)])
                            elif sub >= 1 and j < 2:
                                p.wait("vector", semp("sp", j),
                                       mk_spyh[(sub - 1, 14 + j)])
                            p.op("vector", "dve", lambda q, us=us, dts=dts:
                                 q.tensor_mul(tDTU[:], dts, us))
                            mk_dtu[(sub, j)] = p.n("dve")
                            ypar = j % 2
                            for n in range(16):
                                hpar = n % 2
                                if n == 0:
                                    p.wait("scalar", semp("rd", j))
                                if n >= 2:
                                    p.wait("scalar", "dve",
                                           mk_scan[(sub, j, n - 2)])
                                elif j >= 1 and n < 2:
                                    p.wait("scalar", "dve",
                                           mk_scan[(sub, j - 1, 14 + n)])
                                elif sub >= 1 and j == 0 and n < 2:
                                    p.wait("scalar", "dve",
                                           mk_scan[(sub - 1, 15, 14 + n)])
                                p.op("scalar", "act",
                                     lambda q, j=j, n=n, hpar=hpar, dts=dts:
                                     q.activation(
                                         tDA[:, hpar * 512:(hpar + 1) * 512],
                                         dts, AF.Exp,
                                         scale=cA[:, j * 16 + n:
                                                  j * 16 + n + 1]))
                                mk_dA[(sub, j, n)] = p.n("act")
                                p.op("vector", "dve", lambda q, n=n, hpar=hpar:
                                     q.tensor_mul(
                                         tDBU[:, hpar * 512:(hpar + 1) * 512],
                                         tDTU[:],
                                         tBC[:, n * 512:(n + 1) * 512]))
                                p.wait("vector", "act", mk_dA[(sub, j, n)])
                                p.op("vector", "dve",
                                     lambda q, j=j, n=n, hpar=hpar:
                                     q.tensor_tensor_scan(
                                         tHH[:, hpar * 512:(hpar + 1) * 512],
                                         tDA[:, hpar * 512:(hpar + 1) * 512],
                                         tDBU[:, hpar * 512:(hpar + 1) * 512],
                                         carry[:, j * 16 + n: j * 16 + n + 1],
                                         ALU.mult, ALU.add))
                                mk_scan[(sub, j, n)] = p.n("dve")
                                p.op("vector", "dve",
                                     lambda q, j=j, n=n, hpar=hpar:
                                     q.tensor_copy(
                                         carry[:, j * 16 + n: j * 16 + n + 1],
                                         tHH[:, hpar * 512 + 511:
                                             hpar * 512 + 512]))
                                if n == 0:
                                    p.op("vector", "dve",
                                         lambda q, hpar=hpar, ypar=ypar:
                                         q.tensor_mul(
                                             tYH[:, ypar * 512:
                                                 (ypar + 1) * 512],
                                             tHH[:, hpar * 512:
                                                 (hpar + 1) * 512],
                                             tBC[:, 16 * 512: 17 * 512]))
                                else:
                                    p.op("vector", "dve",
                                         lambda q, n=n, hpar=hpar:
                                         q.tensor_mul(
                                             tHC[:],
                                             tHH[:, hpar * 512:
                                                 (hpar + 1) * 512],
                                             tBC[:, (16 + n) * 512:
                                                 (17 + n) * 512]))
                                    p.op("vector", "dve",
                                         lambda q, ypar=ypar: q.tensor_add(
                                             tYH[:, ypar * 512:
                                                 (ypar + 1) * 512],
                                             tYH[:, ypar * 512:
                                                 (ypar + 1) * 512],
                                             tHC[:]))
                            p.wait("scalar", "dve")
                            p.dma("scalar", semp("sp", j),
                                  sp_yh[j, :, t0:t0 + 512],
                                  tYH[:, ypar * 512:(ypar + 1) * 512])
                            mk_spyh[(sub, j)] = p.n(semp("sp", j))

                    with allocs(nc.sbuf_tensor("tP16", [128, 16], F32), nc.sbuf_tensor("tPP", [128, 256], F32)) as (tP16, tPP,):
                        for j in range(ND):
                            p.wait("vector", "act")
                            p.op("vector", "dve", lambda q, j=j: q.tensor_scalar(
                                tP16[:], cA[:, j * 16:(j + 1) * 16],
                                dcarry[:, j: j + 1], None, ALU.mult))
                            p.wait("scalar", "dve")
                            p.op("scalar", "act", lambda q, j=j: q.activation(
                                tPP[:, j * 16:(j + 1) * 16], tP16[:], AF.Exp))
                        p.wait("scalar", "act")
                        p.wait("scalar", "dve")
                        for j in range(ND):
                            p.dma("scalar", "sp",
                                  cc_in[j * 128:(j + 1) * 128, 0:16],
                                  tPP[:, j * 16:(j + 1) * 16])
                            p.dma("scalar", "sp",
                                  cc_in[j * 128:(j + 1) * 128, 16:32],
                                  carry[:, j * 16:(j + 1) * 16])
                        p.wait("gpsimd", "sp")
                        p.wait("gpsimd", "sp1")
                        p.cnt["cc"] = 1
                        p.emit("gpsimd", lambda q: q.collective_compute(
                            "AllGather", ALU.bypass,
                            replica_groups=[list(range(NCORE))],
                            ins=[cc_in[:, :].opt()],
                            outs=[cc_out[:, :].opt()]).then_inc(S["cc"], 1))
                        p.flush(block)

            with allocs(nc.sbuf_tensor("tCCR", [128, 8 * 512], F32), nc.sbuf_tensor("tGG", [128, 256], F32), nc.sbuf_tensor("tGT", [128, 256], F32)) as (tCCR, tGG, tGT,):
                p.wait("sync", "cc", 1)
                for r in range(NCORE):
                    p.dma("sync", "lc",
                          tCCR[:, r * 512:(r + 1) * 512].rearrange(
                              "p (j c) -> p j c", c=32),
                          cc_out[r * D:(r + 1) * D, :].rearrange(
                              "(j p) c -> p j c", p=128))
                p.wait("vector", "lc")

                def pview(r):
                    return tCCR[:, r * 512:(r + 1) * 512].rearrange(
                        "p (j c) -> p j c", c=32)[:, :, 0:16]

                def sview(r):
                    return tCCR[:, r * 512:(r + 1) * 512].rearrange(
                        "p (j c) -> p j c", c=32)[:, :, 16:32]

                g3 = tGG[:].rearrange("p (j c) -> p j c", c=16)
                t3 = tGT[:].rearrange("p (j c) -> p j c", c=16)
                p.op("vector", "dve", lambda q: q.tensor_copy(g3, sview(0)))
                p.op("vector", "dve", lambda q: q.tensor_scalar(
                    hinA[:], tGG[:], cSELQ[:, 0:1], None, ALU.mult))
                hin, other = hinA, hinB
                for r in range(1, 7):
                    p.op("vector", "dve", lambda q, r=r: q.tensor_tensor(
                        t3, pview(r), g3, ALU.mult))
                    p.op("vector", "dve", lambda q, r=r: q.tensor_tensor(
                        g3, sview(r), t3, ALU.add))
                    p.op("vector", "dve", lambda q, r=r, src=hin, dst=other:
                         q.scalar_tensor_tensor(
                             dst[:], tGG[:], cSELQ[:, r: r + 1], src[:],
                             ALU.mult, ALU.add))
                    hin, other = other, hin
                HIN = hin
                p.flush(block)

            p.wait("scalar", "sp")
            p.wait("scalar", "sp1")
            with allocs(
                nc.sbuf_tensor("tC2", [128, 16 * 512], F16),
                nc.sbuf_tensor("tCC16", [16, 512], F16),
                nc.sbuf_tensor("tYH2", [128, 2 * 512], F16),
                nc.sbuf_tensor("tD2", [128, 2 * 512], F32),
                nc.sbuf_tensor("tU2", [128, 2 * 512], F16),
                nc.sbuf_tensor("tZ2", [128, 2 * 512], F16),
                nc.sbuf_tensor("tCD", [128, 2 * 512], F16),
                nc.sbuf_tensor("tT1", [128, 512], F16),
                nc.sbuf_tensor("tT2", [128, 512], F16),
                nc.sbuf_tensor("tSZ", [128, 2 * 512], F16),
                nc.sbuf_tensor("tYP", [128, 2 * 512], F16),
                nc.psum_tensor("psA2", [128, 512], F32),
                nc.psum_tensor("psB2", [128, 512], F32),
            ) as (tC2, tCC16, tYH2, tD2, tU2, tZ2, tCD, tT1, tT2, tSZ, tYP, psA2, psB2,):
                mk_t1 = {}
                mk_yp2 = {}
                mk_spyp = {}
                for sub in range(NSUB):
                    t0 = sub * TSUB
                    if sub == 0:
                        p.wait("scalar", "dve")
                    p.dma("scalar", "lc", tCC16[:], sp_C[:, t0:t0 + 512])
                    p.wait("tensor", "lc")
                    for n in range(16):
                        ps = psA2 if n % 2 == 0 else psB2
                        if n >= 2:
                            p.wait("tensor", "act")
                        p.op("tensor", "pe", lambda q, n=n, ps=ps:
                             q.matmul(
                                 ps[:], tSEL[:, n * 128:(n + 1) * 128],
                                 tCC16[:], start=True, stop=True))
                        p.wait("scalar", "pe")
                        p.op("scalar", "act", lambda q, n=n, ps=ps: q.copy(
                            tC2[:, n * 512:(n + 1) * 512], ps[:]))
                    mk_c2 = p.n("act")

                    for j in range(ND):
                        par = j % 2
                        yh = tYH2[:, par * 512:(par + 1) * 512]
                        dd = tD2[:, par * 512:(par + 1) * 512]
                        uu = tU2[:, par * 512:(par + 1) * 512]
                        zz = tZ2[:, par * 512:(par + 1) * 512]
                        sz = tSZ[:, par * 512:(par + 1) * 512]
                        if (sub, j - 2) in mk_yp2:
                            p.wait("scalar", "dve", mk_yp2[(sub, j - 2)])
                            p.wait("scalar", semp("sp", j),
                                   mk_spyp[(sub, j - 2)])
                        elif sub >= 1 and j < 2:
                            p.wait("scalar", "dve",
                                   mk_yp2[(sub - 1, 14 + j)])
                            p.wait("scalar", semp("sp", j),
                                   mk_spyp[(sub - 1, 14 + j)])
                        p.dma("scalar", semp("rd", j), yh,
                              sp_yh[j, :, t0:t0 + 512])
                        p.dma("scalar", semp("rd", j), dd,
                              sp_D[j, :, t0:t0 + 512])
                        p.dma("scalar", semp("rd", j), uu,
                              sp_u[j, :, t0:t0 + 512])
                        p.dma("scalar", semp("rd", j), zz,
                              sp_z[j, :, t0:t0 + 512])
                        p.wait("scalar", semp("rd", j))
                        p.wait("vector", semp("rd", j))
                        p.wait("vector", "act", mk_c2)
                        for n in range(16):
                            cpar = n % 2
                            cd = tCD[:, cpar * 512:(cpar + 1) * 512]
                            if (sub, j, n - 2) in mk_t1:
                                p.wait("scalar", "dve",
                                       mk_t1[(sub, j, n - 2)])
                            elif j >= 1 and n < 2:
                                p.wait("scalar", "dve",
                                       mk_t1[(sub, j - 1, 14 + n)])
                            elif sub >= 1 and j == 0 and n < 2:
                                p.wait("scalar", "dve",
                                       mk_t1[(sub - 1, 15, 14 + n)])
                            p.op("scalar", "act",
                                 lambda q, j=j, n=n, cd=cd, dd=dd:
                                 q.activation(cd, dd, AF.Exp,
                                              scale=cA[:, j * 16 + n:
                                                       j * 16 + n + 1]))
                            mk_cd = p.n("act")
                            p.wait("vector", "act", mk_cd)
                            p.op("vector", "dve", lambda q, j=j, n=n, cd=cd:
                                 q.tensor_scalar(
                                     tT1[:], cd,
                                     HIN[:, j * 16 + n: j * 16 + n + 1],
                                     None, ALU.mult))
                            mk_t1[(sub, j, n)] = p.n("dve")
                            p.op("vector", "dve", lambda q, n=n:
                                 q.tensor_mul(tT2[:], tT1[:],
                                              tC2[:, n * 512:(n + 1) * 512]))
                            p.op("vector", "dve", lambda q, yh=yh:
                                 q.tensor_add(yh, yh, tT2[:]))
                        p.op("scalar", "act", lambda q, zz=zz, sz=sz:
                             q.activation(sz, zz, AF.Silu))
                        mk_sz = p.n("act")
                        p.op("vector", "dve", lambda q, uu=uu, j=j:
                             q.tensor_scalar(tT1[:], uu, cDS[:, j: j + 1],
                                             None, ALU.mult))
                        p.op("vector", "dve", lambda q, yh=yh:
                             q.tensor_add(tT2[:], tT1[:], yh))
                        p.wait("vector", "act", mk_sz)
                        if (sub, j - 2) in mk_spyp:
                            p.wait("vector", semp("sp", j),
                                   mk_spyp[(sub, j - 2)])
                        elif sub >= 1 and j < 2:
                            p.wait("vector", semp("sp", j),
                                   mk_spyp[(sub - 1, 14 + j)])
                        p.op("vector", "dve", lambda q, par=par, sz=sz:
                             q.tensor_mul(tYP[:, par * 512:(par + 1) * 512],
                                          tT2[:], sz))
                        mk_yp2[(sub, j)] = p.n("dve")
                        p.wait("scalar", "dve", mk_yp2[(sub, j)])
                        p.dma("scalar", semp("sp", j),
                              sp_yp[j, :, t0:t0 + 512],
                              tYP[:, par * 512:(par + 1) * 512])
                        mk_spyp[(sub, j)] = p.n(semp("sp", j))
                p.flush(block)

            with allocs(
                nc.sbuf_tensor("tYPA", [128, 16 * 512], F16),
                nc.sbuf_tensor("tMO", [128, 4 * 2048], F16),
                nc.sbuf_tensor("tX1", [128, 2 * 2048], F16),
                nc.psum_tensor("psQ0", [128, 1024], F32),
                nc.psum_tensor("psQ1", [128, 1024], F32),
                nc.psum_tensor("psQ2", [128, 1024], F32),
                nc.psum_tensor("psQ3", [128, 1024], F32),
            ) as (tYPA, tMO, tX1, psQ0, psQ1, psQ2, psQ3,):
                psQ = [psQ0, psQ1, psQ2, psQ3]
                p.wait("scalar", "sp")
                p.wait("scalar", "sp1")
                for sub in range(NSUB):
                    t0 = sub * TSUB
                    if sub >= 1:
                        p.wait("scalar", "dve")
                        p.wait("scalar", "pe")
                    for j in range(ND):
                        p.dma("scalar", semp("rd", sub),
                              tYPA[:, j * 512:(j + 1) * 512],
                              sp_yp[j, :, t0:t0 + 512])
                    rd_yp = p.n(semp("rd", sub))
                    for d2h in range(2):
                        for jg in range(4):
                            wpar = jg % 2
                            ws = tW[:, wpar * 4096: wpar * 4096 + 4096]
                            if d2h * 4 + jg >= 2:
                                p.wait("gpsimd", "pe")
                            p.dma("gpsimd", semp("wt", d2h * 4 + jg),
                                  ws,
                                  w_outT[:, (d2h * 4 + jg) * 4096:
                                         (d2h * 4 + jg + 1) * 4096])
                            p.wait("tensor", semp("wt", d2h * 4 + jg))
                            p.wait("tensor", semp("rd", sub), rd_yp)
                            if jg == 0:
                                p.wait("tensor", "act")
                            for ji in range(4):
                                j = jg * 4 + ji
                                for it in range(4):
                                    for nh in range(2):
                                        mmfn = (lambda q, j=j, ji=ji, it=it,
                                                nh=nh, ws=ws: q.matmul(
                                                    psQ[it][:, nh * 512:
                                                            (nh + 1) * 512],
                                                    tYPA[:, j * 512 + it * 128:
                                                         j * 512 +
                                                         (it + 1) * 128],
                                                    ws[:, ji * 1024 + nh * 512:
                                                       ji * 1024 +
                                                       (nh + 1) * 512],
                                                    start=(j == 0),
                                                    stop=(j == ND - 1)))
                                        if ji == 3 and it == 3 and nh == 1:
                                            p.op("tensor", "pe", mmfn)
                                        else:
                                            p.opn("tensor", mmfn)
                        p.wait("scalar", "pe")
                        for it in range(4):
                            p.op("scalar", "act", lambda q, it=it, d2h=d2h:
                                 q.copy(tMO[:, it * 2048 + d2h * 1024:
                                            it * 2048 + (d2h + 1) * 1024],
                                        psQ[it][:]))
                    mo_mk = p.n("act")

                    for it in range(4):
                        tt = sub * 4 + it
                        xpar = it % 2
                        xs = tX[:, xpar * 2048:(xpar + 1) * 2048]
                        mo = tMO[:, it * 2048:(it + 1) * 2048]
                        x1 = tX1[:, xpar * 2048:(xpar + 1) * 2048]
                        if it >= 2:
                            p.wait("gpsimd", "dve")
                        p.dma("gpsimd", semp("lx", it), xs,
                              xin[t0 + it * 128: t0 + (it + 1) * 128, :])
                        p.wait("vector", "act", mo_mk)
                        p.op("vector", "dve", lambda q, mo=mo: q.reduce_sum(
                            tSTAT[:, 0:1], mo, axis=mybir.AxisListType.X))
                        p.op("scalar", "act", lambda q, mo=mo: q.activation(
                            tJA[:], mo, AF.Square, scale=cKON[:, 4:5]))
                        p.wait("vector", "act")
                        p.op("vector", "dve", lambda q: q.reduce_sum(
                            tSTAT[:, 1:2], tJA[:], axis=mybir.AxisListType.X))
                        p.op("vector", "dve", lambda q: q.tensor_scalar(
                            tSTAT[:, 2:3], tSTAT[:, 0:1], RD, None, ALU.mult))
                        p.wait("vector", "dve")
                        p.op("vector", "dve", lambda q: q.scalar_tensor_tensor(
                            tSTAT[:, 4:5], tSTAT[:, 2:3], tSTAT[:, 2:3],
                            tSTAT[:, 1:2], ALU.mult, ALU.subtract))
                        p.wait("scalar", "dve")
                        p.op("scalar", "act", lambda q: q.activation(
                            tSTAT[:, 5:6], tSTAT[:, 4:5], AF.Ln,
                    scale=cKON[:, 3:4], bias=cKON[:, 0:1]))
                        p.wait("scalar", "act")
                        p.op("scalar", "act", lambda q: q.activation(
                            tSTAT[:, 6:7], tSTAT[:, 5:6], AF.Exp, scale=cKON[:, 2:3]))
                        p.wait("vector", "act")
                        p.op("vector", "dve", lambda q, mo=mo: q.tensor_scalar(
                            mo, mo, tSTAT[:, 2:3], tSTAT[:, 6:7],
                            ALU.subtract, ALU.mult))
                        p.wait("vector", semp("lx", it))
                        if tt >= 2:
                            p.wait("vector", semp("sp", tt))
                        p.op("vector", "dve", lambda q, x1=x1, xs=xs, mo=mo:
                             q.tensor_add(x1, xs, mo))
                        x1_mk = p.n("dve")
                        p.wait("scalar", "dve", x1_mk)
                        p.dma("scalar", semp("sp", tt), sp_x1[tt, :, :], x1)
                        p.op("vector", "dve", lambda q, x1=x1: q.reduce_sum(
                            tSTAT[:, 0:1], x1, axis=mybir.AxisListType.X))
                        p.op("scalar", "act", lambda q, x1=x1: q.activation(
                            tJA[:], x1, AF.Square, scale=cKON[:, 4:5]))
                        p.wait("vector", "act")
                        p.op("vector", "dve", lambda q: q.reduce_sum(
                            tSTAT[:, 1:2], tJA[:], axis=mybir.AxisListType.X))
                        p.op("vector", "dve", lambda q: q.tensor_scalar(
                            tSTAT[:, 2:3], tSTAT[:, 0:1], RD, None, ALU.mult))
                        p.wait("vector", "dve")
                        p.op("vector", "dve", lambda q: q.scalar_tensor_tensor(
                            tSTAT[:, 4:5], tSTAT[:, 2:3], tSTAT[:, 2:3],
                            tSTAT[:, 1:2], ALU.mult, ALU.subtract))
                        p.wait("scalar", "dve")
                        p.op("scalar", "act", lambda q: q.activation(
                            tSTAT[:, 5:6], tSTAT[:, 4:5], AF.Ln,
                    scale=cKON[:, 3:4], bias=cKON[:, 0:1]))
                        p.wait("scalar", "act")
                        p.op("scalar", "act", lambda q: q.activation(
                            tSTAT[:, 6:7], tSTAT[:, 5:6], AF.Exp, scale=cKON[:, 2:3]))
                        p.wait("vector", "act")
                        p.wait("vector", "tr")
                        p.op("vector", "dve", lambda q, x1=x1:
                             q.tensor_scalar(
                                 tLNTM[:, 0:2048],
                                 x1, tSTAT[:, 2:3], tSTAT[:, 6:7],
                                 ALU.subtract, ALU.mult))
                        p.wait("sync", "dve")
                        for j in range(ND):
                            p.trp(BIGA[:, j * 2048 + tt * 128:
                                       j * 2048 + (tt + 1) * 128],
                                  tLNTM[:, j * 128:(j + 1) * 128])
                p.flush(block)

            with allocs(
                nc.sbuf_tensor("tES", [128, 2 * 2048], F16),
                nc.psum_tensor("psM0", [128, 512], F32),
                nc.psum_tensor("psM1", [128, 512], F32),
                nc.psum_tensor("psM2", [128, 512], F32),
                nc.psum_tensor("psM3", [128, 512], F32),
                nc.psum_tensor("psE0", [128, 512], F32),
                nc.psum_tensor("psE1", [128, 512], F32),
            ) as (tES, psM0, psM1, psM2, psM3, psE0, psE1,):
                _ffn = ExitStack()
                tDO = _ffn.enter_context(
                    nc.sbuf_tensor("tDO", [128, 2 * 4 * 2048], F16))
                tW1 = _ffn.enter_context(
                    nc.sbuf_tensor("tW1", [128, 2 * 8192], F16))
                psM = [psM0, psM1, psM2, psM3]
                psE = [psE0, psE1]
                ea_prev = (0, 0)
                mk_m1grp = {}
                mk_m1ev = {}
                mk_mm2last = {}
                mk_m2grp = {}
                mk_m2ev = {}
                mk_ea = {}
                for oct_ in range(16):
                    dslot = (oct_ % 2) * 8192
                    wslot = (oct_ % 2) * 8192
                    if oct_ >= 2:
                        p.wait("gpsimd", "pe", mk_mm2last[oct_ - 2])
                    p.dma("gpsimd", semp("lx", oct_), tW1[:, wslot:wslot + 8192],
                          w1T[:, oct_ * 8192:(oct_ + 1) * 8192])
                    mk_w1ld = p.n(semp("lx", oct_))
                    for hh in range(4):
                        i = oct_ * 4 + hh
                        h = oct_ * 4 + hh
                        wpar = i % 2
                        ws = tW[:, wpar * 4096: wpar * 4096 + 2048]
                        if i >= 2:
                            p.wait("gpsimd", "pe", mk_m1grp[i - 2])
                        p.dma("gpsimd", semp("wt", i), ws,
                              w0T[:, h * 2048:(h + 1) * 2048])
                        p.wait("tensor", semp("wt", i))
                        if i == 0:
                            p.wait("tensor", "tr")
                        for quad in range(4):
                            pq = psM[quad]
                            if i >= 1:
                                p.wait("tensor", "act",
                                       mk_m1ev[(i - 1, quad)])
                            for k in range(ND):
                                mmfn = (lambda q, k=k, quad=quad, pq=pq, ws=ws:
                                        q.matmul(
                                            pq[:],
                                            ws[:, k * 128:(k + 1) * 128],
                                            BIGA[:, k * 2048 + quad * 512:
                                                 k * 2048 + (quad + 1) * 512],
                                            start=(k == 0), stop=(k == ND - 1)))
                                if k == ND - 1:
                                    p.op("tensor", "pe", mmfn)
                                else:
                                    p.opn("tensor", mmfn)
                            if quad == 3:
                                mk_m1grp[i] = p.n("pe")
                            p.wait("scalar", "pe")
                            if oct_ >= 2 and hh == 0 and quad == 0:
                                p.wait("scalar", "pe", mk_mm2last[oct_ - 2])
                            p.op("scalar", "act",
                                 lambda q, hh=hh, quad=quad, pq=pq, h=h,
                                 dslot=dslot:
                                 q.activation(
                                     tDO[:, dslot + hh * 2048 + quad * 512:
                                         dslot + hh * 2048 + (quad + 1) * 512],
                                     pq[:], AF.Gelu,
                                     bias=cB0[:, h: h + 1]))
                            mk_m1ev[(i, quad)] = p.n("act")
                    mk_do = p.n("act")
                    if oct_ >= 1:
                        p.wait("gpsimd", "ea", ea_prev[0])
                        p.wait("gpsimd", "ea1", ea_prev[1])
                    p.wait("tensor", semp("lx", oct_), mk_w1ld)
                    for d2c in range(16):
                        p.wait("tensor", "act", mk_do)
                        for tq in range(4):
                            idx = d2c * 4 + tq
                            pq = psE[idx % 2]
                            if idx >= 2:
                                p.wait("tensor", "act",
                                       mk_m2ev[(oct_, idx - 2)])
                            elif oct_ >= 1:
                                p.wait("tensor", "act",
                                       mk_m2ev[(oct_ - 1, 62 + idx)])
                            for hh in range(4):
                                mmfn = (lambda q, hh=hh, tq=tq, pq=pq,
                                        dslot=dslot, wslot=wslot, d2c=d2c:
                                        q.matmul(
                                            pq[:],
                                            tW1[:, wslot + hh * 2048 +
                                                d2c * 128:
                                                wslot + hh * 2048 +
                                                (d2c + 1) * 128],
                                            tDO[:, dslot + hh * 2048 + tq * 512:
                                                dslot + hh * 2048 +
                                                (tq + 1) * 512],
                                            start=(hh == 0), stop=(hh == 3)))
                                if hh == 3:
                                    p.op("tensor", "pe", mmfn)
                                else:
                                    p.opn("tensor", mmfn)
                            p.wait("scalar", "pe")
                            eslot = d2c % 2
                            if tq == 0 and (oct_, d2c - 2) in mk_ea:
                                p.wait("scalar", semp("ea", d2c),
                                       mk_ea[(oct_, d2c - 2)])
                            elif tq == 0 and oct_ >= 1 and d2c < 2:
                                p.wait("scalar", semp("ea", d2c),
                                       mk_ea[(oct_ - 1, 14 + d2c)])
                            p.op("scalar", "act",
                                 lambda q, tq=tq, eslot=eslot,
                                 pq2=psE[idx % 2]:
                                 q.copy(tES[:, eslot * 2048 + tq * 512:
                                            eslot * 2048 + (tq + 1) * 512],
                                        pq2[:]))
                            mk_m2ev[(oct_, idx)] = p.n("act")
                        mk_m2grp[(oct_, d2c)] = p.n("pe")
                        mk_es = p.n("act")
                        p.wait("gpsimd", "act", mk_es)
                        p.dma("gpsimd", semp("ea", d2c), sp_E[d2c, :, :],
                              tES[:, (d2c % 2) * 2048:
                                  (d2c % 2 + 1) * 2048],
                              **({} if oct_ == 0 else
                                 dict(accum_op=ALU.add)))
                        mk_ea[(oct_, d2c)] = p.n(semp("ea", d2c))
                    mk_mm2last[oct_] = p.n("pe")
                    ea_prev = (p.n("ea"), p.n("ea1"))

                p.flush(block)
                _ffn.close()

                with allocs(nc.sbuf_tensor("tETM", [128, 4 * 2048], F16), nc.sbuf_tensor("tEQ", [128, 2 * 512], F16), nc.sbuf_tensor("tX1R", [128, 2 * 2048], F16)) as (tETM, tEQ, tX1R,):
                    p.wait("scalar", "ea")
                    p.wait("scalar", "ea1")
                    p.wait("scalar", "sp")
                    p.wait("scalar", "sp1")
                    p.wait("sync", "ea")
                    p.wait("sync", "ea1")
                    for ttg in range(4):
                        if ttg >= 1:
                            p.wait("scalar", "dve")
                        for d2c in range(16):
                            epar = d2c % 2
                            eq = tEQ[:, epar * 512:(epar + 1) * 512]
                            if d2c >= 2:
                                p.wait("scalar", semp("tr", d2c))
                            p.dma("scalar", semp("rd", d2c), eq,
                                  sp_E[d2c, :, ttg * 512:(ttg + 1) * 512])
                            p.wait("vector", semp("rd", d2c))
                            p.op("vector", "dve", lambda q, eq=eq, d2c=d2c:
                                 q.tensor_scalar(eq, eq, cB1[:, d2c: d2c + 1],
                                                 None, ALU.add))
                            p.wait("sync", "dve")
                            for g in range(4):
                                p.trp(tETM[:, g * 2048 + d2c * 128:
                                           g * 2048 + (d2c + 1) * 128],
                                      eq[:, g * 128:(g + 1) * 128],
                                      sem=semp("tr", d2c))
                        p.wait("vector", "tr")
                        p.wait("vector", "tr1")
                        p.wait("scalar", "tr")
                        p.wait("scalar", "tr1")
                        for g in range(4):
                            tt = ttg * 4 + g
                            em = tETM[:, g * 2048:(g + 1) * 2048]
                            xpar = g % 2
                            x1r = tX1R[:, xpar * 2048:(xpar + 1) * 2048]
                            if g >= 2:
                                p.wait("scalar", "dve")
                            p.dma("scalar", semp("xr", g), x1r,
                                  sp_x1[tt, :, :])
                            p.op("vector", "dve", lambda q, em=em:
                                 q.reduce_sum(tSTAT[:, 0:1], em,
                                              axis=mybir.AxisListType.X))
                            p.op("scalar", "act", lambda q, em=em:
                                 q.activation(tJA[:], em, AF.Square, scale=cKON[:, 4:5]))
                            p.wait("vector", "act")
                            p.op("vector", "dve", lambda q: q.reduce_sum(
                                tSTAT[:, 1:2], tJA[:],
                                axis=mybir.AxisListType.X))
                            p.op("vector", "dve", lambda q: q.tensor_scalar(
                                tSTAT[:, 2:3], tSTAT[:, 0:1], RD, None,
                                ALU.mult))
                            p.wait("vector", "dve")
                            p.op("vector", "dve", lambda q: q.scalar_tensor_tensor(
                                tSTAT[:, 4:5], tSTAT[:, 2:3], tSTAT[:, 2:3],
                                tSTAT[:, 1:2], ALU.mult, ALU.subtract))
                            p.wait("scalar", "dve")
                            p.op("scalar", "act", lambda q: q.activation(
                                tSTAT[:, 5:6], tSTAT[:, 4:5], AF.Ln,
                    scale=cKON[:, 3:4], bias=cKON[:, 0:1]))
                            p.wait("scalar", "act")
                            p.op("scalar", "act", lambda q: q.activation(
                                tSTAT[:, 6:7], tSTAT[:, 5:6], AF.Exp, scale=cKON[:, 2:3]))
                            p.wait("vector", "act")
                            p.op("vector", "dve", lambda q, em=em:
                                 q.tensor_scalar(em, em, tSTAT[:, 2:3],
                                                 tSTAT[:, 6:7],
                                                 ALU.subtract, ALU.mult))
                            p.wait("vector", semp("xr", g))
                            if tt >= 2:
                                p.wait("vector", "ou", 16 * (tt - 1))
                            osl = tOUT[:, (tt % 2) * 2048:
                                       (tt % 2 + 1) * 2048]
                            p.op("vector", "dve", lambda q, em=em, x1r=x1r,
                                 osl=osl:
                                 q.tensor_add(osl, x1r, em))
                            p.wait("sync", "dve")
                            p.dma("sync", "ou",
                                  out[tt * 128:(tt + 1) * 128, :], osl)
                    p.wait("sync", "ou")
                    p.flush(block)

    return nc


_CACHED = {}


def _prep_inputs(inputs):
    x = np.ascontiguousarray(
        np.asarray(inputs["L0_attn_x"], dtype=np.float32).reshape(16384, D))
    W_in = np.asarray(inputs["L0_attn_W_in"], dtype=np.float32)
    conv_w = np.asarray(inputs["L0_attn_conv_w"], dtype=np.float32)
    conv_b = np.asarray(inputs["L0_attn_conv_b"], dtype=np.float32)
    W_xp = np.asarray(inputs["L0_attn_W_xproj"], dtype=np.float32)
    W_dt = np.asarray(inputs["L0_attn_W_dt"], dtype=np.float32)
    b_dt = np.asarray(inputs["L0_attn_b_dt"], dtype=np.float32)
    A_log = np.asarray(inputs["L0_attn_A_log"], dtype=np.float32)
    Dskip = np.asarray(inputs["L0_attn_Dskip"], dtype=np.float32)
    W_out = np.asarray(inputs["L0_attn_W_out"], dtype=np.float32)
    W0 = np.asarray(inputs["L0_ffn_W0"], dtype=np.float32)
    b0 = np.asarray(inputs["L0_ffn_bias0"], dtype=np.float32)
    W1 = np.asarray(inputs["L0_ffn_W1"], dtype=np.float32)
    b1 = np.asarray(inputs["L0_ffn_bias1"], dtype=np.float32)

    shared = {
        "w_inT": np.ascontiguousarray(
            W_in.T.reshape(16, 128, 32, 128).transpose(1, 2, 0, 3)
            .reshape(128, 32 * 2048)),
        "w_xpT": np.ascontiguousarray(
            W_xp.T.reshape(16, 128, 160).transpose(1, 0, 2).reshape(128, 16 * 160)),
        "w_dtT": np.ascontiguousarray(W_dt.T),
        "w_outT": np.ascontiguousarray(
            W_out.T.reshape(4, 4, 128, 2, 1024).transpose(2, 3, 0, 1, 4)
            .reshape(128, 32 * 1024)),
        "w0T": np.ascontiguousarray(
            W0.T.reshape(16, 128, 64, 128).transpose(1, 2, 0, 3)
            .reshape(128, 64 * 2048)),
        "w1T": np.ascontiguousarray(
            W1.T.reshape(16, 4, 128, 2048).transpose(2, 0, 1, 3)
            .reshape(128, 16 * 8192)),
        "c_alog": np.ascontiguousarray(
            A_log.reshape(16, 128, 16).transpose(1, 0, 2).reshape(128, 256)),
        "c_cw": np.ascontiguousarray(
            conv_w.reshape(16, 128, 4).transpose(1, 0, 2).reshape(128, 64)),
        "c_cb": np.ascontiguousarray(conv_b.reshape(16, 128).T),
        "c_bdt": np.ascontiguousarray(b_dt.reshape(16, 128).T),
        "c_ds": np.ascontiguousarray(Dskip.reshape(16, 128).T),
        "c_b0": np.ascontiguousarray(b0.reshape(64, 128).T),
        "c_b1": np.ascontiguousarray(b1.reshape(16, 128).T),
    }

    in_maps = []
    for c in range(NCORE):
        m = dict(shared)
        m["xin"] = np.ascontiguousarray(x[c * TC:(c + 1) * TC])
        halo = np.zeros((16, D), dtype=np.float32)
        if c > 0:
            halo[13:16] = x[c * TC - 3: c * TC]
        m["xhalo"] = halo
        csel = np.zeros((128, 8), dtype=np.float32)
        if c > 0:
            csel[:, c - 1] = 1.0
        m["c_csel"] = csel
        in_maps.append(m)
    return in_maps


def kernel(**inputs):
    if "nc" not in _CACHED:
        _CACHED["nc"] = build_nc()
    nc = _CACHED["nc"]
    in_maps = _prep_inputs(inputs)
    res = run_bass_kernel_spmd(nc, in_maps, core_ids=list(range(NCORE)))
    outp = np.concatenate([res.results[c]["out"] for c in range(NCORE)], axis=0)
    return outp.reshape(1, 16384, D).astype(np.float32)
